# version 14
# baseline (speedup 1.0000x reference)
import numpy as np
from contextlib import ExitStack

import concourse.bass as bass
import concourse.mybir as mybir
from concourse.bass_utils import run_bass_kernel_spmd

F32 = mybir.dt.float32
F32R = mybir.dt.float32r
AF = mybir.ActivationFunctionType
ALU = mybir.AluOpType
AX = mybir.AxisListType

NT = 2048
D = 1024
NDT = 8
COL = 2060
WIN = 256
NW = 8
NL = 6
DI = 2816
NJC = 22
NEG = -60000.0
SHUF = [x for i in range(16) for x in (2 * i + 1, 2 * i)]
ENGINES = ("sync", "tensor", "vector", "scalar", "gpsimd")


class PB:

    def __init__(self, nc, es):
        self.nc = nc
        self.es = es
        self.steps = {e: [] for e in ENGINES}
        self.sems = {}
        self.cnt = {}
        self.last_w = {}
        self.readers = {}
        self.dma_rr = {}
        self.preludes = {}

    def _sem(self, eng, dma):
        if dma:
            pool = 10 if eng == "sync" else 2
            idx = self.dma_rr.get(eng, 0)
            self.dma_rr[eng] = idx + 1
            name = "d" + eng[0].upper() + str(idx % pool)
        else:
            name = "c" + eng[0].upper()
        if name not in self.sems:
            self.sems[name] = self.es.enter_context(self.nc.semaphore(name))
            self.cnt[name] = 0
        return name

    def deps_of(self, *bufs):
        out = []
        for b in bufs:
            t = self.last_w.get(b)
            if t:
                out.append(t)
            out.extend(self.readers.get(b, ()))
        return out

    def op(self, eng, fn, reads=(), writes=(), wnd=(), extra=(), dma=False):
        sname = self._sem(eng, dma)
        deps = list(extra)
        if dma and self.cnt[sname] > 0:
            deps.append((sname, self.cnt[sname]))
        for b in reads:
            t = self.last_w.get(b)
            if t:
                deps.append(t)
        for b in writes:
            t = self.last_w.get(b)
            if t:
                deps.append(t)
            deps.extend(self.readers.get(b, ()))
        waits, drain = {}, False
        for (s, v) in deps:
            if waits.get(s, -1) < v:
                waits[s] = v
        inc = 16 if dma else 1
        self.cnt[sname] += inc
        tok = (sname, self.cnt[sname])
        self.steps[eng].append((fn, tuple(sorted(waits.items())), drain, sname, inc))
        for b in reads:
            self.readers.setdefault(b, []).append(tok)
        for b in writes:
            self.last_w[b] = tok
            self.readers[b] = []
        for b in wnd:
            self.last_w[b] = tok
            self.readers[b] = []
        return tok

    def t(self, fn, **kw):
        return self.op("tensor", fn, **kw)

    def v(self, fn, **kw):
        return self.op("vector", fn, **kw)

    def s(self, fn, **kw):
        return self.op("scalar", fn, **kw)

    def g(self, fn, **kw):
        return self.op("gpsimd", fn, **kw)

    def dma(self, fn, eng="sync", **kw):
        return self.op(eng, fn, dma=True, **kw)

    def fence(self, old_bufs, new_bufs, smslice):
        deps = self.deps_of(*old_bufs)
        tok = self.op("vector", lambda E: E.memset(smslice, 0.0), extra=deps)
        for b in new_bufs:
            self.last_w[b] = tok
            self.readers[b] = []

    def pe_barrier(self):
        self.steps["tensor"].append((None, (), True, None, 0))

    def final_wait(self, eng, toks):
        wd = {}
        for (s, v) in toks:
            if wd.get(s, -1) < v:
                wd[s] = v
        self.steps[eng].append((None, tuple(sorted(wd.items())), False, None, 0))

    def emit(self):
        block = self.es.enter_context(self.nc.Block())
        for eng in ENGINES:
            steps = self.steps[eng]
            if not steps:
                continue
            section = getattr(block, eng)

            def body(E, steps=steps, eng=eng):
                pre = self.preludes.get(eng)
                if pre:
                    pre(E)
                seen = {}
                for fn, waits, drain, sname, inc in steps:
                    if drain:
                        E.drain()
                    for (s, v) in waits:
                        if seen.get(s, -1) < v:
                            E.wait_ge(self.sems[s], v)
                            seen[s] = v
                    if fn is None:
                        continue
                    fn(E).then_inc(self.sems[sname], inc)
            section(body)


def kcs_of(w):
    return (2, 3) if w == 0 else (0, 1, 2, 3)


def build_program(n_layers=NL, do_rwkv=True, do_final=True, dbg=(), qkv_bf16=True):
    nc = bass.Bass(trn_type="TRN2", target_bir_lowering=False)
    es = ExitStack()

    def din(name, shape):
        return nc.dram_tensor(name, shape, F32, kind="ExternalInput").ap()

    x_h = din("x", [NT, D])
    out_h = nc.dram_tensor("out", [NT, D], F32, kind="ExternalOutput").ap()
    x1sp = nc.dram_tensor("x1sp", [128, NDT * COL], F32, kind="Internal").ap()
    wosp = nc.dram_tensor("wosp", [128, NW * 2048], F32, kind="Internal").ap()

    wqkv_h = din("wqkv", [NL, D, 3 * D])
    wo_h = din("wo", [NL, D, D])
    w1_h = din("w1", [NL, D, 2 * DI])
    w1b_h = nc.dram_tensor("w1b", [NL, D, 4 * DI], mybir.dt.bfloat16, kind="ExternalInput").ap()
    w2b_h = nc.dram_tensor("w2b", [NL, DI, 2 * D], mybir.dt.bfloat16, kind="ExternalInput").ap()
    wob_h = nc.dram_tensor("wob", [NL, D, 2 * D], mybir.dt.bfloat16, kind="ExternalInput").ap()
    w2_h = din("w2", [NL, DI, D])
    b1c_h = din("b1c", [NL, 128, 2 * NJC])
    lng_h = din("lng", [NL, 128, NDT])
    lnbf_h = din("lnbf", [NL, 128, NDT])
    b2_h = din("b2", [NL, 128, NDT])
    qs_h = din("qs", [NL, 128, 1])
    qss_h = din("qss", [NL, 128, 1])
    ks_h = din("ks", [NL, 128, 1])
    kss_h = din("kss", [NL, 128, 1])
    qsrow_h = din("qsrow", [NL, 1, 64])
    ksrow_h = din("ksrow", [NL, 1, 64])
    c1q_h = din("c1q", [128, NT])
    c2q_h = din("c2q", [128, NT])
    c1k_h = din("c1k", [128, NT])
    c2k_h = din("c2k", [128, NT])
    e8_h = din("e8", [128, 4, 8])
    r8_h = din("r8", [8, 4, 128])
    ind2_h = din("ind2", [1, 2, 128])
    onesr_h = din("onesr", [1, 128])
    onesc_h = din("onesc", [128, 1])
    ident_h = din("ident", [128, 128])
    kvwt_h = din("kvwt", [D, 128])
    owt_h = din("owt", [64, D])
    td_h = din("td", [64, 1])
    tf_h = din("tf", [64, 1])
    gates_h = din("gatesf", [128, NDT])
    lnw_h = din("lnw", [128, NDT])
    lnb_h = din("lnb", [128, NDT])

    dbg_out = {}
    for name in dbg:
        shp = [128, NDT * COL] if name[0] in "xzl" else [128, 8192]
        dbg_out[name] = nc.dram_tensor("dbg_" + name, shp, F32,
                                       kind="ExternalOutput").ap()

    AZ = es.enter_context(nc.sbuf_tensor("AZ", [128, NDT * COL], F32))
    SH = es.enter_context(nc.sbuf_tensor("SH", [128, 30752], F32))
    SM = es.enter_context(nc.sbuf_tensor("SM", [128, 4352], F32))
    ID = es.enter_context(nc.sbuf_tensor("ID", [128, 128], F32))
    E8 = es.enter_context(nc.sbuf_tensor("E8", [128, 32], F32))
    R8 = es.enter_context(nc.sbuf_tensor("R8", [128, 512], F32))
    ONESC = es.enter_context(nc.sbuf_tensor("ONESC", [128, 1], F32))
    pb = [es.enter_context(nc.psum_tensor(f"pb{i}", [128, 512], F32)) for i in range(8)]
    PBN = tuple(f"pb{i}" for i in range(8))

    def az(dt_, c0, n):
        return AZ[:, dt_ * COL + c0: dt_ * COL + c0 + n]

    def sh(off, n, p0=0, p1=128):
        return SH[p0:p1, off: off + n]

    SHB = SH[:].bitcast(mybir.dt.bfloat16)

    def shb(base_w, eoff, n, p0=0, p1=128):
        return SHB[p0:p1, 2 * base_w + eoff: 2 * base_w + eoff + n]

    def sm(off, n, p0=0, p1=128):
        return SM[p0:p1, off: off + n]

    ONESR = SM[0:1, 3584 + 96: 3584 + 224]
    INDLO = SM[0:1, 3584 + 224: 3584 + 352]
    INDHI = SM[0:1, 3584 + 352: 3584 + 480]

    O_WQ, O_WK, O_WV, O_WO = 0, 4096, 8192, 12288
    O_KR, O_VR, O_QS, O_SS, O_OS = 16384, 18432, 20512, 21536, 25632
    O_WOH = 26656
    F_MB, F_NT = 0, 11264
    F_W1S, F_W1H, F_W2S, F_W2H = 19456, 23552, 27648, 28672
    R_Y, R_K, R_V, R_WKV, R_ST, R_C = 0, 16384, 18432, 20480, 22528, 24576
    S_T1, S_T2, S_TAB = 0, 512, 1024
    S_STA, S_STB, S_STC = 2048, 2560, 3072
    S_BIAS, S_MISC = 3584, 4096

    regs = {}

    def gps_prelude(E):
        regs["neg"] = E.alloc_register("negfill")
        E.reg_mov(regs["neg"], NEG)

    ATTN_BUFS = ("wq", "wk", "wv", "wo", "kr0", "kr1", "vr0", "vr1", "vr2", "vr3",
                 "qsb", "ss", "zsp0", "zsp1", "woh", "wol", "os0", "os1", "os2", "os3",
                 "tA1q", "tA2q", "tA1k", "tA2k")
    FFN_BUFS = ("mb", "ntq0", "ntq1", "w1b0", "w1b1", "w2b0", "w2c1", "w1h0", "w1h1", "w2h0", "w2h1")
    RWKV_BUFS = ("y", "rk", "rv", "rwkvb", "rst", "rwc")
    AZB = tuple(f"az{d}" for d in range(NDT))

    P = PB(nc, es)
    P.preludes["gpsimd"] = gps_prelude

    P.dma(lambda E: E.dma_start(out=ID[:], in_=ident_h), writes=("ID",))
    P.dma(lambda E: E.dma_start(out=E8[:, 0:32].rearrange("p (a b) -> p a b", a=4),
                                in_=e8_h), writes=("E8",))
    P.dma(lambda E: E.dma_start(out=R8[0:8, 0:512].rearrange("p (a b) -> p a b", a=4),
                                in_=r8_h), writes=("R8",))
    P.dma(lambda E: E.dma_start(out=INDLO, in_=ind2_h[:, 0, :]), writes=("IND",))
    P.dma(lambda E: E.dma_start(out=INDHI, in_=ind2_h[:, 1, :]), writes=("IND",))
    P.dma(lambda E: E.dma_start(out=ONESR, in_=onesr_h), writes=("ONESR",))
    P.dma(lambda E: E.dma_start(out=ONESC[:], in_=onesc_h), writes=("ONESC",))

    P.v(lambda E: E.memset(AZ[:], 0.0), writes=AZB)
    for j in range(16):
        slot = j % 2
        xa = sh(slot * 1024, 1024)
        xb = ("wq", "wk")[slot]
        P.dma(lambda E, j=j, xa=xa: E.dma_start(out=xa, in_=x_h[128 * j:128 * (j + 1), :]),
              writes=(xb,))
        for g4 in range(2):
            bank = PBN[g4]
            for q in range(4):
                P.t(lambda E, xa=xa, dt_=g4 * 4 + q, q=q, g4=g4:
                    E.transpose(pb[g4][:, 128 * q:128 * (q + 1)],
                                xa[:, 128 * dt_:128 * (dt_ + 1)], ID[:]),
                    reads=(xb, "ID"), writes=(bank,) if q == 0 else (), wnd=(bank,) if q else ())
            dst = AZ[:].rearrange("p (d c) -> p d c", d=NDT)[:, g4 * 4:(g4 + 1) * 4,
                                                            1 + 128 * j: 129 + 128 * j]
            P.s(lambda E, dst=dst, g4=g4:
                E.copy(dst, pb[g4][:].rearrange("p (a b) -> p a b", a=4)),
                reads=(bank,), writes=(AZB[g4 * 4], AZB[g4 * 4 + 1], AZB[g4 * 4 + 2], AZB[g4 * 4 + 3]))

    def ln_block(read_slice, n_cols, write_fn, affine=None, eps_only_rstd=False, tag=""):
        srcs = [read_slice(dt_) for dt_ in range(NDT)]
        for dt_ in range(NDT):
            ap_, b_ = srcs[dt_]
            P.t(lambda E, ap_=ap_, dt_=dt_, n=n_cols:
                E.matmul(pb[4][0:1, 0:n], ONESC[:], ap_, start=(dt_ == 0), stop=(dt_ == NDT - 1)),
                reads=(b_, "ONESC"), writes=("pb4",) if dt_ == 0 else (), wnd=("pb4",) if dt_ else ())
            sqb = ("sqA", "sqB")[dt_ % 2]
            sqa = sm((S_T1, S_T2)[dt_ % 2], n_cols)
            P.s(lambda E, ap_=ap_, sqa=sqa: E.activation(sqa, ap_, AF.Square),
                reads=(b_,), writes=(sqb,))
            P.t(lambda E, sqa=sqa, dt_=dt_, n=n_cols:
                E.matmul(pb[5][0:1, 0:n], ONESC[:], sqa, start=(dt_ == 0), stop=(dt_ == NDT - 1)),
                reads=(sqb, "ONESC"), writes=("pb5",) if dt_ == 0 else (), wnd=("pb5",) if dt_ else ())
        mu = sm(S_STA, n_cols, 0, 1)
        v1 = sm(S_STB, n_cols, 0, 1)
        v2 = sm(S_STC, n_cols, 0, 1)
        sd = v2
        rstd = v1
        P.v(lambda E: E.tensor_single_scalar(mu, pb[4][0:1, 0:n_cols], 1.0 / D, ALU.mult),
            reads=("pb4",), writes=("stA",))
        P.v(lambda E: E.tensor_single_scalar(v1, pb[5][0:1, 0:n_cols], 1.0 / D, ALU.mult),
            reads=("pb5",), writes=("stB",))
        P.v(lambda E: E.tensor_tensor(v2, mu, mu, ALU.mult), reads=("stA",), writes=("stC",))
        P.v(lambda E: E.tensor_tensor(v1, v1, v2, ALU.subtract), reads=("stC",), writes=("stB",))
        P.v(lambda E: E.tensor_single_scalar(v1, v1, 1e-5, ALU.add), reads=(), writes=("stB",))
        P.s(lambda E: E.activation(sd, v1, AF.Sqrt), reads=("stB",), writes=("stC",))
        P.v(lambda E: E.reciprocal(rstd, sd), reads=("stC",), writes=("stB",))
        P.t(lambda E, n=n_cols: E.matmul(pb[6][:, 0:n], ONESR, mu, start=True, stop=True),
            reads=("stA", "ONESR"), writes=("pb6",))
        P.t(lambda E, n=n_cols: E.matmul(pb[7][:, 0:n], ONESR, rstd, start=True, stop=True),
            reads=("stB", "ONESR"), writes=("pb7",))
        for dt_ in range(NDT):
            write_fn(dt_, srcs[dt_])

    if do_rwkv:
        rc = R_C
        P.dma(lambda E: E.dma_start(out=sh(rc, 1024).rearrange("p (k m) -> p k m", k=8),
                                    in_=kvwt_h.rearrange("(k p) m -> p k m", p=128)),
              writes=("rwc",), extra=P.deps_of("wq", "wk"))
        P.dma(lambda E: E.dma_start(out=sh(rc + 1024, 1024, 0, 64), in_=owt_h), wnd=("rwc2",))
        for nm, hbm, off in (("gts", gates_h, 0), ("lnw", lnw_h, 8), ("lnb", lnb_h, 16)):
            P.dma(lambda E, hbm=hbm, off=off: E.dma_start(out=sm(S_BIAS + off, 8), in_=hbm),
                  writes=(nm,))
        P.dma(lambda E: E.dma_start(out=sm(S_MISC + 16, 1, 0, 64), in_=tf_h), writes=("tf",))
        P.dma(lambda E: E.dma_start(out=sm(S_MISC + 17, 1, 0, 64), in_=td_h), writes=("td",))
        gcol = sm(S_MISC + 0, 8)
        omg = sm(S_MISC + 8, 8)
        P.s(lambda E: E.activation(gcol, sm(S_BIAS, 8), AF.Sigmoid), reads=("gts",), writes=("gcol",))
        P.v(lambda E: E.tensor_scalar(omg, gcol, -1.0, 1.0, ALU.mult, ALU.add),
            reads=("gcol",), writes=("omg",))
        tfe = sm(S_MISC + 18, 1, 0, 64)
        ntf = sm(S_MISC + 19, 1, 0, 64)
        wcol = sm(S_MISC + 20, 1, 0, 64)
        P.s(lambda E: E.activation(tfe, sm(S_MISC + 16, 1, 0, 64), AF.Exp), reads=("tf",), writes=("tfe",))
        P.v(lambda E: E.tensor_single_scalar(ntf, tfe, -1.0, ALU.mult), reads=("tfe",), writes=("ntf",))
        P.s(lambda E: E.activation(wcol, sm(S_MISC + 17, 1, 0, 64), AF.Exp), reads=("td",), writes=("wcol",))

        for blk in range(4):
            c0 = 1 + 512 * blk

            def wr(dt_, src, blk=blk, c0=c0):
                ap_, b_ = src
                ya = sh(R_Y + dt_ * 2048 + 512 * blk, 512)
                t1 = sm(S_T1, 512)
                P.v(lambda E, ap_=ap_, t1=t1: E.tensor_tensor(t1, ap_, pb[6][:, 0:512], ALU.subtract),
                    reads=(b_, "pb6"), writes=("sqA",))
                P.v(lambda E, ya=ya, t1=t1: E.tensor_tensor(ya, t1, pb[7][:, 0:512], ALU.mult),
                    reads=("sqA", "pb7"), writes=("y",), extra=P.deps_of("wq", "wk") if (dt_ == 0 and blk == 0) else ())
            ln_block(lambda dt_, c0=c0: (az(dt_, c0, 512), AZB[dt_]), 512, wr)

        for dt_ in range(NDT):
            gc = sm(S_MISC + 0 + dt_, 1)
            oc = sm(S_MISC + 8 + dt_, 1)
            for seg in (0, 512):
                lo = sh(R_Y + dt_ * 2048 + seg, 512)
                up = sh(R_Y + dt_ * 2048 + 1024 + seg, 512)
                t1 = sm(S_T1, 512)
                t2 = sm(S_T2, 512)
                P.v(lambda E, t1=t1, lo=lo: E.tensor_copy(t1, lo), reads=("y",), writes=("sqA",))
                P.v(lambda E, t2=t2, t1=t1, oc=oc: E.tensor_single_scalar(t2, t1, oc, ALU.mult),
                    reads=("sqA", "omg"), writes=("sqB",))
                P.v(lambda E, lo=lo, up=up, gc=gc, t2=t2:
                    E.scalar_tensor_tensor(lo, up, gc, t2, ALU.mult, ALU.add),
                    reads=("y", "sqB", "gcol"), writes=("y",))
                P.v(lambda E, t2=t2, up=up, oc=oc: E.tensor_single_scalar(t2, up, oc, ALU.mult),
                    reads=("y", "omg"), writes=("sqB",))
                P.v(lambda E, up=up, t1=t1, gc=gc, t2=t2:
                    E.scalar_tensor_tensor(up, t1, gc, t2, ALU.mult, ALU.add),
                    reads=("sqA", "sqB", "gcol"), writes=("y",))

        for blk in range(4):
            def wr2(dt_, src, blk=blk):
                ap_, b_ = src
                t1 = sm(S_T1, 512)
                gc = sm(S_BIAS + 8 + dt_, 1)
                bc = sm(S_BIAS + 16 + dt_, 1)
                P.v(lambda E, ap_=ap_, t1=t1: E.tensor_tensor(t1, ap_, pb[6][:, 0:512], ALU.subtract),
                    reads=(b_, "pb6"), writes=("sqA",))
                P.v(lambda E, ap_=ap_, t1=t1: E.tensor_tensor(t1, t1, pb[7][:, 0:512], ALU.mult),
                    reads=("sqA", "pb7"), writes=("sqA",))
                P.v(lambda E, ap_=ap_, t1=t1, gc=gc, bc=bc:
                    E.tensor_scalar(ap_, t1, gc, bc, ALU.mult, ALU.add),
                    reads=("sqA", "lnw", "lnb"), writes=("y",))
            ln_block(lambda dt_, blk=blk: (sh(R_Y + dt_ * 2048 + 512 * blk, 512), "y"), 512, wr2)

        for nb in range(4):
            for which, col0, dstoff, dstb in (("k", 0, R_K, "rk"), ("v", 64, R_V, "rv")):
                bank = PBN[nb % 2]
                for kd in range(NDT):
                    P.t(lambda E, kd=kd, nb=nb, col0=col0:
                        E.matmul(pb[nb % 2][0:64, 0:512],
                                 sh(rc + kd * 128 + col0, 64),
                                 sh(R_Y + kd * 2048 + 512 * nb, 512),
                                 start=(kd == 0), stop=(kd == NDT - 1)),
                        reads=("y", "rwc"), writes=(bank,) if kd == 0 else (), wnd=(bank,) if kd else ())
                P.s(lambda E, nb=nb, dstoff=dstoff:
                    E.copy(sh(dstoff + 512 * nb, 512, 0, 64), pb[nb % 2][0:64, 0:512]),
                    reads=(bank,), writes=(dstb,))
        P.s(lambda E: E.activation(sh(R_WKV, 2048, 0, 64), sh(R_K, 2048, 0, 64), AF.Exp,
                                   scale=ntf), reads=("rk", "ntf"), writes=("rwkvb",))
        P.v(lambda E: E.tensor_tensor(sh(R_WKV, 2048, 0, 64), sh(R_WKV, 2048, 0, 64),
                                      sh(R_V, 2048, 0, 64), ALU.mult),
            reads=("rwkvb", "rv"), writes=("rwkvb",))
        P.v(lambda E: E.tensor_tensor_scan(sh(R_ST, 2048, 0, 64),
                                           wcol.broadcast_to([64, 2048]),
                                           sh(R_WKV, 2048, 0, 64), 0.0,
                                           ALU.mult, ALU.add),
            reads=("rwkvb", "wcol"), writes=("rst",))
        for dc in range(NDT):
            for nb in range(4):
                bank = PBN[(dc * 4 + nb) % 2]
                P.t(lambda E, dc=dc, nb=nb:
                    E.matmul(pb[(dc * 4 + nb) % 2][:, 0:512],
                             sh(rc + 1024 + 128 * dc, 128, 0, 64),
                             sh(R_ST + 512 * nb, 512, 0, 64), start=True, stop=True),
                    reads=("rst", "rwc2"), writes=(bank,))
                P.v(lambda E, dc=dc, nb=nb:
                    E.scalar_tensor_tensor(az(dc, 1 + 512 * nb, 512),
                                           pb[(dc * 4 + nb) % 2][:, 0:512], 1.0,
                                           az(dc, 1 + 512 * nb, 512), ALU.mult, ALU.add),
                    reads=(bank, AZB[dc]), writes=(AZB[dc],))

        spill_tok = P.dma(lambda E: E.dma_start(out=x1sp, in_=AZ[:]), reads=AZB, writes=("x1sp",))

        if "x1" in dbg_out:
            P.dma(lambda E: E.dma_start(out=dbg_out["x1"], in_=AZ[:]), reads=AZB, writes=("dbgx1",))

        for blk in range(4):
            def wr3(dt_, src, blk=blk):
                ap_, b_ = src
                t1 = sm(S_T1, 512)
                P.v(lambda E, ap_=ap_, t1=t1: E.tensor_tensor(t1, ap_, pb[6][:, 0:512], ALU.subtract),
                    reads=(b_, "pb6"), writes=("sqA",))
                P.v(lambda E, ap_=ap_, t1=t1: E.tensor_tensor(ap_, t1, pb[7][:, 0:512], ALU.mult),
                    reads=("sqA", "pb7", "x1sp"), writes=(AZB[dt_],))
            ln_block(lambda dt_, blk=blk: (az(dt_, 1 + 512 * blk, 512), AZB[dt_]), 512, wr3)

    if "z0" in dbg_out:
        P.dma(lambda E: E.dma_start(out=dbg_out["z0"], in_=AZ[:]), reads=AZB, writes=("dbgz0",))

    for l in range(n_layers):
        P.fence(RWKV_BUFS + ("rwc2",) + FFN_BUFS + ("wq", "wk"), ATTN_BUFS, sm(S_MISC + 168, 1))
        lc = []
        for nm, hbm, off, wid in (("qscol", qs_h, 0, 1), ("qsscol", qss_h, 1, 1),
                                  ("kscol", ks_h, 2, 1), ("ksscol", kss_h, 3, 1)):
            P.dma(lambda E, hbm=hbm, off=off, l=l: E.dma_start(out=sm(S_MISC + 32 + off, 1), in_=hbm[l]),
                  writes=(nm,))
        P.dma(lambda E, l=l: E.dma_start(out=sm(S_MISC + 36, 64, 0, 1), in_=qsrow_h[l]), writes=("qsrow",))
        P.dma(lambda E, l=l: E.dma_start(out=sm(S_MISC + 100, 64, 0, 1), in_=ksrow_h[l]), writes=("ksrow",))
        P.dma(lambda E, l=l: E.dma_start(out=sm(S_BIAS + 24, 2 * NJC), in_=b1c_h[l]), writes=("b1c",))
        P.dma(lambda E, l=l: E.dma_start(out=sm(S_BIAS + 68, NDT), in_=lng_h[l]), writes=("lngc",))
        P.dma(lambda E, l=l: E.dma_start(out=sm(S_BIAS + 76, NDT), in_=lnbf_h[l]), writes=("lnbfc",))
        P.dma(lambda E, l=l: E.dma_start(out=sm(S_BIAS + 84, NDT), in_=b2_h[l]), writes=("b2c",))
        m1 = sm(S_MISC + 164, 1, 0, 1)
        m2 = sm(S_MISC + 165, 1, 0, 1)
        P.v(lambda E: E.reduce_max(m1, sm(S_MISC + 36, 64, 0, 1), AX.X, apply_absolute_value=True),
            reads=("qsrow",), writes=("m1",))
        P.v(lambda E: E.reduce_max(m2, sm(S_MISC + 100, 64, 0, 1), AX.X, apply_absolute_value=True),
            reads=("ksrow",), writes=("m2",))
        P.v(lambda E: E.tensor_tensor(m1, m1, m2, ALU.mult), reads=("m2",), writes=("m1",))
        P.v(lambda E: E.tensor_single_scalar(m1, m1, -8.0, ALU.mult), reads=(), writes=("m1",))
        P.t(lambda E: E.matmul(pb[2][:, 0:1], ONESR, m1, start=True, stop=True),
            reads=("m1", "ONESR"), writes=("pb2",))
        negc = sm(S_MISC + 166, 1)
        P.s(lambda E: E.activation(negc, pb[2][:, 0:1], AF.Copy), reads=("pb2",), writes=("negc",))

        for hg in range(2):
            for wname, base, creg in (("wq", O_WQ, 0), ("wk", O_WK, 1024), ("wv", O_WV, 2048)):
                for kd in range(NDT):
                    scr = sm((S_T1, S_T2, S_STA, S_STB)[kd % 4], 512)
                    scrb = ("sqA", "sqB", "stA", "stB")[kd % 4]
                    P.dma(lambda E, kd=kd, creg=creg, hg=hg, l=l, scr=scr:
                          E.dma_start(out=scr,
                                      in_=wqkv_h[l, 128 * kd:128 * kd + 128,
                                                 creg + 512 * hg:creg + 512 * hg + 512]),
                          writes=(scrb,))
                    P.s(lambda E, kd=kd, base=base, scr=scr:
                        E.copy(shb(base, 512 * kd, 512), scr),
                        reads=(scrb,), writes=(wname,) if kd == 0 else (), wnd=(wname,))
                    P.v(lambda E, kd=kd, base=base, scr=scr:
                        E.tensor_tensor(shb(base, 4096 + 512 * kd, 512), scr,
                                        shb(base, 512 * kd, 512), ALU.subtract),
                        reads=(scrb, wname), wnd=(wname,))
            P.dma(lambda E, hg=hg, l=l: E.dma_start(
                out=sh(O_WO, 4096).rearrange("p (k m) -> p k m", k=4),
                in_=wo_h[l, 512 * hg:512 * hg + 512, :].rearrange("(k p) m -> p k m", p=128)),
                writes=("wo",), extra=P.deps_of("zsp0", "zsp1"))
            P.s(lambda E: E.copy(shb(O_WOH, 0, 4096), sh(O_WO, 4096)),
                reads=("wo",), writes=("woh",))
            P.v(lambda E: E.tensor_tensor(shb(O_WOH + 2048, 0, 4096), sh(O_WO, 4096),
                                          shb(O_WOH, 0, 4096), ALU.subtract),
                reads=("wo", "woh"), writes=("wol",))
            P.fence(("wo",), ("zsp0", "zsp1"), sm(S_MISC + 173, 1))
            vr_ones = SH[:, O_VR:O_VR + 2080].rearrange("p (s h c) -> p s h c", s=4, h=8)[:, :, :, 64:65]
            P.v(lambda E: E.memset(vr_ones, 1.0),
                writes=("vr0", "vr1", "vr2", "vr3"))

            def zs_slice(dt_, c0, n):
                off = 0 if dt_ < 4 else -1
                return az(dt_, c0 + off, n)

            def emit_zsp(wn):
                slot = wn % 2
                zb = O_WO + 2048 * slot
                cuz = 1 + WIN * wn
                zn = f"zsp{slot}"
                for kd in range(NDT):
                    P.s(lambda E, kd=kd, cuz=cuz, zb=zb:
                        E.copy(shb(zb, 256 * kd, 256), zs_slice(kd, cuz, 256)),
                        reads=(AZB[kd],) + (("wo",) if kd == 0 else ()),
                        writes=(zn,) if kd == 0 else (), wnd=(zn,))
                    P.g(lambda E, kd=kd, cuz=cuz, zb=zb:
                        E.tensor_tensor(shb(zb, 2048 + 256 * kd, 256),
                                        zs_slice(kd, cuz, 256),
                                        shb(zb, 256 * kd, 256), ALU.subtract),
                        reads=(AZB[kd], zn), wnd=(zn,))

            emit_zsp(0)

            for w in range(NW):
                cu = 1 + WIN * w
                kcs = kcs_of(w)

                for nm, hbm, toff, scol, sb_ in (("A1q", c1q_h, 0, S_MISC + 32, "qscol"),
                                                 ("A2q", c2q_h, 256, S_MISC + 33, "qsscol"),
                                                 ("A1k", c1k_h, 512, S_MISC + 34, "kscol"),
                                                 ("A2k", c2k_h, 768, S_MISC + 35, "ksscol")):
                    P.dma(lambda E, hbm=hbm, toff=toff, w=w:
                          E.dma_start(out=sm(S_TAB + toff, 256), in_=hbm[:, WIN * w:WIN * w + 256]),
                          writes=("t" + nm,))
                    P.v(lambda E, toff=toff, scol=scol:
                        E.tensor_single_scalar(sm(S_TAB + toff, 256), sm(S_TAB + toff, 256),
                                               sm(scol, 1), ALU.mult),
                        reads=("t" + nm, sb_), writes=("t" + nm,))


                zb = O_WO + 2048 * (w % 2)
                zn = f"zsp{w % 2}"
                for which, wbuf, wname in (("q", O_WQ, "wq"), ("k", O_WK, "wk")):
                    scol0 = 0 if which == "q" else 256

                    def emit_e8(i, scol0=scol0):
                        sqa = sm((S_T1, S_T2)[i % 2], 256)
                        P.t(lambda E, i=i, sqa=sqa, scol0=scol0:
                            E.matmul(pb[2][0:8, scol0:scol0 + 256], E8[:, 8 * i:8 * i + 8], sqa,
                                     start=(i == 0), stop=(i == 3)),
                            reads=(("sqA", "sqB")[i % 2], "E8"), writes=("pb2",) if i == 0 else (),
                            wnd=("pb2",) if i else ())

                    for i in range(4):
                        bank = PBN[i % 2]
                        nmm = 0
                        for kd in range(NDT):
                            for wd, xd in ((0, 0), (0, 2048), (4096, 0)):
                                P.t(lambda E, i=i, kd=kd, wbuf=wbuf, wd=wd, xd=xd, nmm=nmm, zb=zb:
                                    E.matmul(pb[i % 2][:, 0:256],
                                             shb(wbuf, wd + kd * 512 + 128 * i, 128),
                                             shb(zb, xd + kd * 256, 256),
                                             start=(nmm == 0), stop=(nmm == 23)),
                                    reads=(wname, zn), writes=(bank,) if nmm == 0 else (),
                                    wnd=(bank,) if nmm else ())
                                nmm += 1
                        dst = sh(O_QS + 256 * i, 256) if which == "q" else sh(O_KR + (w % 2) * 256 + i * 512, 256)
                        dstb = "qsb" if which == "q" else f"kr{w % 2}"
                        P.s(lambda E, dst=dst, i=i: E.copy(dst, pb[i % 2][:, 0:256]),
                            reads=(bank,), writes=(dstb,) if i == 0 else (), wnd=(dstb,) if i else ())
                        sqa = sm((S_T1, S_T2)[i % 2], 256)
                        P.g(lambda E, sqa=sqa, dst=dst: E.tensor_tensor(sqa, dst, dst, ALU.mult),
                            reads=(dstb,), writes=(("sqA", "sqB")[i % 2],))
                        if i >= 1:
                            emit_e8(i - 1)
                    emit_e8(3)
                    rsa = sm(S_STA + scol0, 256, 0, 8)
                    P.s(lambda E, rsa=rsa, scol0=scol0:
                        E.activation(rsa, pb[2][0:8, scol0:scol0 + 256], AF.Sqrt),
                        reads=("pb2",), writes=("stA",))
                    P.v(lambda E, rsa=rsa: E.tensor_scalar_max(rsa, rsa, 1e-12),
                        reads=(), writes=("stA",))
                    P.v(lambda E, rsa=rsa: E.reciprocal(rsa, rsa), reads=(), writes=("stA",))

                def emit_rot(which, i):
                    scol0 = 0 if which == "q" else 256
                    rbk = 6 if which == "q" else 7
                    rsa = sm(S_STA + scol0, 256, 0, 8)
                    t_a1 = S_TAB + (0 if which == "q" else 512)
                    t_a2 = S_TAB + (256 if which == "q" else 768)
                    a1b = "tA1q" if which == "q" else "tA1k"
                    a2b = "tA2q" if which == "q" else "tA2k"
                    P.t(lambda E, i=i, rsa=rsa, rbk=rbk:
                        E.matmul(pb[rbk][:, 0:256], R8[0:8, 128 * i:128 * (i + 1)], rsa,
                                 start=True, stop=True),
                        reads=("stA", "R8"), writes=(PBN[rbk],))
                    raww = (O_QS + 256 * i) if which == "q" else (O_KR + (w % 2) * 256 + i * 512)
                    dst = sh(raww, 256)
                    dstb = "qsb" if which == "q" else f"kr{w % 2}"
                    w0 = sm(S_STC + scol0, 256)
                    w0b = "stCq" if which == "q" else "stCk"
                    P.v(lambda E, dst=dst, w0=w0, rbk=rbk: E.tensor_tensor(w0, dst, pb[rbk][:, 0:256], ALU.mult),
                        reads=(dstb, PBN[rbk]), writes=(w0b,))
                    t1 = sm(S_T1, 256)
                    t2 = sm(S_T2, 256)
                    P.v(lambda E, t1=t1, w0=w0: E.stream_shuffle(t1, w0, SHUF),
                        reads=(w0b,), writes=("sqA",))
                    P.v(lambda E, t1=t1, t2=t2, t_a2=t_a2:
                        E.tensor_tensor(t2, t1, sm(t_a2, 256), ALU.mult),
                        reads=("sqA", a2b), writes=("sqB",))
                    P.v(lambda E, w0=w0, t_a1=t_a1:
                        E.tensor_tensor(w0, w0, sm(t_a1, 256), ALU.mult),
                        reads=(w0b, a1b), writes=(w0b,))
                    P.v(lambda E, w0=w0, t2=t2: E.tensor_tensor(w0, w0, t2, ALU.add),
                        reads=("sqB", w0b), writes=(w0b,))
                    P.s(lambda E, raww=raww, w0=w0: E.copy(shb(raww, 0, 256), w0),
                        reads=(w0b,), writes=(dstb,))
                    P.v(lambda E, raww=raww, w0=w0:
                        E.tensor_tensor(shb(raww + 128, 0, 256), w0,
                                        shb(raww, 0, 256), ALU.subtract),
                        reads=(w0b, dstb), wnd=(dstb,))

                nkc = len(kcs)
                kc0 = kcs[0]

                def emit_v_mms(tc, kds, first, last):
                    bank = PBN[tc % 2]
                    for kd in kds:
                        for wd, xd in ((0, 0), (4096, 0), (0, 2048)):
                            P.t(lambda E, tc=tc, kd=kd, wd=wd, xd=xd, zb=zb:
                                E.matmul(pb[tc % 2][:, 0:512],
                                         shb(zb, xd + kd * 256 + 128 * tc, 128),
                                         shb(O_WV, wd + kd * 512, 512),
                                         start=(kd == 0 and wd == 0 and xd == 0),
                                         stop=(kd == NDT - 1 and xd == 2048)),
                                reads=("wv", zn),
                                writes=(bank,) if (first and kd == kds[0] and wd == 0 and xd == 0) else (),
                                wnd=(bank,) if not (first and kd == kds[0] and wd == 0 and xd == 0) else ())

                def emit_v_evac(tc):
                    bank = PBN[tc % 2]
                    vslot = (w % 2) * 2 + tc
                    dstv = SH[:, O_VR + 520 * vslot:O_VR + 520 * vslot + 520]\
                        .rearrange("p (h c) -> p h c", h=8)[:, :, 0:64]
                    P.s(lambda E, dstv=dstv, tc=tc:
                        E.copy(dstv, pb[tc % 2][:, 0:512].rearrange("p (h c) -> p h c", h=8)),
                        reads=(bank,), writes=(f"vr{vslot}",))

                def emit_qk_slot(par, i):
                    sbase = O_SS
                    ssn = "ss"
                    r0 = 64 * par
                    tp = (64 * par, 0)
                    for kc in kcs:
                        b_ = (w - 1 + kc // 2) % 2
                        kbase = O_KR + b_ * 256 + i * 512
                        bank = PBN[4 + kc // 2]
                        for pi2, (wd, xd) in enumerate(((0, 0), (0, 1), (1, 0))):
                            P.t(lambda E, i=i, kc=kc, r0=r0, tp=tp, kbase=kbase,
                                wd=wd, xd=xd, pi2=pi2:
                                E.matmul(pb[4 + kc // 2][:, 256 * (kc % 2):256 * (kc % 2) + 256],
                                         shb(kbase + 128 * wd, 128 * (kc % 2), 128, r0, r0 + 64),
                                         shb(O_QS + 256 * i + 128 * xd, 0, 256, r0, r0 + 64),
                                         start=(pi2 == 0), stop=(pi2 == 2), tile_position=tp),
                                reads=(f"kr{b_}", "qsb"), writes=(bank,) if pi2 == 0 else (),
                                wnd=(bank,) if pi2 else ())
                    for half in ((0, 1) if w else (1,)):
                        bank = PBN[4 + half]
                        first = (i == 0 and half == (0 if w else 1))
                        P.s(lambda E, i=i, half=half, sbase=sbase:
                            E.copy(sh(sbase + 1024 * i + 512 * half, 512),
                                   pb[4 + half][:, 0:512]),
                            reads=(bank,),
                            writes=(ssn,) if first else (),
                            wnd=(ssn,))
                    hoff = sbase + 1024 * i + (0 if w else 512)
                    ap_h = SH[:, hoff:hoff + 256 * nkc].rearrange("p (k q) -> p k q", k=nkc)
                    P.g(lambda E, ap_h=ap_h, nkc=nkc, kc0=kc0:
                        E.affine_select(ap_h, ap_h, [[-128, nkc], [1, 256]],
                                        ALU.is_ge, regs["neg"], base=256 - 128 * kc0,
                                        channel_multiplier=-1),
                        reads=(ssn,), writes=(ssn,))
                    P.g(lambda E, ap_h=ap_h, nkc=nkc, kc0=kc0:
                        E.affine_select(ap_h, ap_h, [[128, nkc], [-1, 256]],
                                        ALU.is_ge, regs["neg"], base=128 * kc0,
                                        channel_multiplier=1),
                        reads=(ssn,), writes=(ssn,))
                    P.s(lambda E, ap_h=ap_h: E.activation(ap_h, ap_h, AF.Exp,
                                                          bias=negc, scale=8.0),
                        reads=(ssn, "negc"), writes=(ssn,))

                def emit_sums_pv(par):
                    ssn = "ss"
                    stv = SH[:, O_SS:O_SS + 4096].rearrange("p (h k q) -> p h k q", h=4, k=4)
                    for gi, (h0, bnk) in enumerate(((0, 2), (2, 3))):
                        for ki, kc in enumerate(kcs):
                            P.t(lambda E, h0=h0, kc=kc, bnk=bnk, ki=ki, nkc=nkc, stv=stv:
                                E.matmul(pb[bnk][0:1, 0:512], ONESC[:],
                                         stv[:, h0:h0 + 2, kc, :],
                                         start=(ki == 0), stop=(ki == nkc - 1)),
                                reads=(ssn, "ONESC"), writes=(PBN[bnk],) if ki == 0 else (),
                                wnd=(PBN[bnk],) if ki else ())
                    recA = sm(S_STB, 512, 0, 1)
                    recB = sm(S_STC, 512, 0, 1)
                    P.v(lambda E: E.reciprocal(recA, pb[2][0:1, 0:512]), reads=("pb2",), writes=("stB",))
                    P.v(lambda E: E.reciprocal(recB, pb[3][0:1, 0:512]), reads=("pb3",), writes=("stC",))
                    for pi in range(2):
                        os_slot = pi if par == 0 else 2 + pi
                        rec = recA if pi == 0 else recB
                        rb = "stB" if pi == 0 else "stC"
                        P.t(lambda E, rec=rec: E.matmul(pb[4][:, 0:256], INDLO,
                                                        rec[:, 0:256], start=True, stop=False),
                            reads=(rb, "IND"), writes=("pb4",))
                        P.t(lambda E, rec=rec: E.matmul(pb[4][:, 0:256], INDHI,
                                                        rec[:, 256:512], start=False, stop=True),
                            reads=(rb, "IND"), wnd=("pb4",))
                        repsb = sm(S_T1, 256)
                        P.s(lambda E, repsb=repsb: E.copy(repsb, pb[4][:, 0:256]),
                            reads=("pb4",), writes=("sqA",))
                        for sub in range(2):
                            slot = 2 * pi + sub
                            hl = par + 2 * slot
                            tp = (0, 64 * sub)
                            prange = (64 * sub, 64 * sub + 64)
                            for ki, kc in enumerate(kcs):
                                b_ = (w - 1 + kc // 2) % 2
                                vslot = b_ * 2 + (kc % 2)
                                P.t(lambda E, hl=hl, kc=kc, vslot=vslot, slot=slot, tp=tp,
                                    prange=prange, ki=ki, nkc=nkc, stv=stv:
                                    E.matmul(pb[5][prange[0]:prange[1], 0:256],
                                             sh(O_VR + 520 * vslot + 65 * hl, 64),
                                             stv[:, slot, kc, :],
                                             start=(ki == 0), stop=(ki == nkc - 1),
                                             tile_position=tp),
                                    reads=(f"vr{vslot}", ssn), writes=("pb5",) if ki == 0 else (),
                                    wnd=("pb5",) if ki else ())
                        ot = sm(S_T2, 256)
                        P.v(lambda E, ot=ot, repsb=repsb:
                            E.tensor_tensor(ot, pb[5][:, 0:256], repsb, ALU.mult),
                            reads=("pb5", "sqA"), writes=("sqB",))
                        P.s(lambda E, os_slot=os_slot, ot=ot:
                            E.copy(shb(O_OS, 256 * os_slot, 256), ot),
                            reads=("sqB",), writes=(f"os{os_slot}",))
                        P.v(lambda E, os_slot=os_slot, ot=ot:
                            E.tensor_tensor(shb(O_OS + 512, 256 * os_slot, 256), ot,
                                            shb(O_OS, 256 * os_slot, 256), ALU.subtract),
                            reads=("sqB", f"os{os_slot}"), wnd=(f"os{os_slot}",))

                emit_rot("q", 0)
                emit_rot("k", 0)
                emit_rot("q", 1)
                emit_rot("k", 1)
                for i in range(4):
                    if i + 2 < 4:
                        emit_rot("q", i + 2)
                        emit_rot("k", i + 2)
                    emit_qk_slot(0, i)
                    emit_v_mms(0, [2 * i, 2 * i + 1], first=(i == 0), last=(i == 3))
                emit_v_evac(0)
                emit_v_mms(1, list(range(NDT)), first=True, last=True)
                emit_v_evac(1)
                if w + 1 < NW:
                    emit_zsp(w + 1)
                emit_sums_pv(0)
                P.pe_barrier()
                for i in range(4):
                    emit_qk_slot(1, i)
                P.pe_barrier()
                emit_sums_pv(1)

                if hg == 1:
                    P.dma(lambda E, w=w: E.dma_start(out=sh(O_SS, 2048), in_=wosp[:, 2048 * w:2048 * (w + 1)]),
                          reads=("wosp",), writes=("ss",))
                for dc in range(NDT):
                    bank = PBN[dc % 2]
                    nmm = 0
                    for kt in range(4):
                        for wd, xd in ((0, 0), (0, 1024), (4096, 0)):
                            P.t(lambda E, dc=dc, kt=kt, wd=wd, xd=xd, nmm=nmm:
                                E.matmul(pb[dc % 2][:, 0:256],
                                         shb(O_WOH, wd + kt * 1024 + 128 * dc, 128),
                                         shb(O_OS, xd + 256 * kt, 256),
                                         start=(nmm == 0), stop=(nmm == 11)),
                                reads=("woh", "wol", f"os{kt}"), writes=(bank,) if nmm == 0 else (),
                                wnd=(bank,) if nmm else ())
                            nmm += 1
                    stage = sh(O_SS + 256 * dc, 256)
                    if hg == 0:
                        P.s(lambda E, stage=stage, dc=dc: E.copy(stage, pb[dc % 2][:, 0:256]),
                            reads=(bank,), writes=("ss",) if dc == 0 else (), wnd=("ss",))
                    else:
                        P.v(lambda E, dc=dc, cu=cu:
                            E.scalar_tensor_tensor(az(dc, cu, 256), pb[dc % 2][:, 0:256], 1.0,
                                                   az(dc, cu, 256), ALU.mult, ALU.add),
                            reads=(bank, AZB[dc]), writes=(AZB[dc],))
                        P.v(lambda E, dc=dc, cu=cu, stage=stage:
                            E.tensor_tensor(az(dc, cu, 256), az(dc, cu, 256), stage, ALU.add),
                            reads=("ss", AZB[dc]), writes=(AZB[dc],))
                if hg == 0:
                    P.dma(lambda E, w=w: E.dma_start(out=wosp[:, 2048 * w:2048 * (w + 1)], in_=sh(O_SS, 2048)),
                          reads=("ss",), writes=("wosp",))

        if f"l{l}a" in dbg_out:
            P.dma(lambda E, l=l: E.dma_start(out=dbg_out[f"l{l}a"], in_=AZ[:]), reads=AZB,
                  writes=(f"dbgl{l}a",))

        P.fence(ATTN_BUFS, FFN_BUFS, sm(S_MISC + 169, 1))

        def emit_ln(tb):
            cu = 1 + 512 * tb
            nslot = tb % 2
            ntb = F_NT + 4096 * nslot
            ntn = f"ntq{nslot}"

            def zs2(dt_, cu=cu):
                off = 0 if dt_ < 4 else -1
                return az(dt_, cu + off, 512)

            def wrn(dt_, src, ntb=ntb, ntn=ntn):
                ap_, b_ = src
                t1 = sm(S_T1, 512)
                t2 = sm(S_T2, 512)
                gc = sm(S_BIAS + 68 + dt_, 1)
                bc = sm(S_BIAS + 76 + dt_, 1)
                P.v(lambda E, ap_=ap_, t1=t1: E.tensor_tensor(t1, ap_, pb[6][:, 0:512], ALU.subtract),
                    reads=(b_, "pb6"), writes=("sqA",))
                P.v(lambda E, t1=t1, t2=t2: E.tensor_tensor(t2, t1, pb[7][:, 0:512], ALU.mult),
                    reads=("sqA", "pb7"), writes=("sqB",))
                P.v(lambda E, t1=t1, t2=t2, gc=gc, bc=bc:
                    E.tensor_scalar(t1, t2, gc, bc, ALU.mult, ALU.add),
                    reads=("sqB", "lngc", "lnbfc"), writes=("sqA",))
                P.s(lambda E, dt_=dt_, t1=t1, ntb=ntb: E.copy(shb(ntb, 512 * dt_, 512), t1),
                    reads=("sqA",), writes=(ntn,) if dt_ == 0 else (), wnd=(ntn,))
                P.v(lambda E, dt_=dt_, t1=t1, ntb=ntb:
                    E.tensor_tensor(shb(ntb + 2048, 512 * dt_, 512), t1,
                                    shb(ntb, 512 * dt_, 512), ALU.subtract),
                    reads=("sqA", ntn), wnd=(ntn,))
            ln_block(lambda dt_: (zs2(dt_), AZB[dt_]), 512, wrn)

        def w1_load(jcp):
            slot = jcp % 2
            P.dma(lambda E, jcp=jcp, slot=slot, l=l:
                  E.dma_start(out=sh(F_W1S + 2048 * slot, 2048).rearrange("p (k m) -> p k m", k=8),
                              in_=w1_h[l, :, 256 * jcp:256 * jcp + 256].rearrange("(k p) m -> p k m", p=128)),
                  writes=(f"w1b{slot}",))
            P.s(lambda E, slot=slot: E.copy(shb(F_W1H + 2048 * slot, 0, 2048),
                                            sh(F_W1S + 2048 * slot, 2048)),
                reads=(f"w1b{slot}",), writes=(f"w1h{slot}",))
            P.v(lambda E, slot=slot:
                E.tensor_tensor(shb(F_W1H + 2048 * slot + 1024, 0, 2048),
                                sh(F_W1S + 2048 * slot, 2048),
                                shb(F_W1H + 2048 * slot, 0, 2048), ALU.subtract),
                reads=(f"w1b{slot}", f"w1h{slot}"), wnd=(f"w1h{slot}",))

        def w2_load(jc):
            slot = jc % 2
            stg = sh(F_W2S, 1024) if slot == 0 else sm(S_TAB, 1024)
            stn = "w2b0" if slot == 0 else "w2c1"
            P.dma(lambda E, jc=jc, l=l, stg=stg:
                  E.dma_start(out=stg,
                              in_=w2_h[l, 128 * jc:128 * jc + 128, :]),
                  writes=(stn,))
            P.s(lambda E, slot=slot, stg=stg: E.copy(shb(F_W2H + 1024 * slot, 0, 1024),
                                                     stg),
                reads=(stn,), writes=(f"w2h{slot}",))
            P.v(lambda E, slot=slot, stg=stg:
                E.tensor_tensor(shb(F_W2H + 1024 * slot + 512, 0, 1024),
                                stg,
                                shb(F_W2H + 1024 * slot, 0, 1024), ALU.subtract),
                reads=(stn, f"w2h{slot}"), wnd=(f"w2h{slot}",))

        emit_ln(0)
        w1_load(0)
        w1_load(1)
        for tb in range(4):
            cu = 1 + 512 * tb
            nslot = tb % 2
            ntb = F_NT + 4096 * nslot
            ntn = f"ntq{nslot}"

            for jcp in range(NJC):
                slot = jcp % 2
                for half in range(2):
                    bk = 2 * slot + half
                    bank = PBN[bk]
                    nmm = 0
                    for kd in range(NDT):
                        for wd, xd in ((0, 0), (0, 4096), (2048, 0)):
                            P.t(lambda E, slot=slot, bk=bk, kd=kd, wd=wd, xd=xd, nmm=nmm, ntb=ntb:
                                E.matmul(pb[bk][:, 0:512],
                                         shb(F_W1H + 2048 * slot, wd + 256 * kd + 128 * (bk % 2), 128),
                                         shb(ntb, xd + 512 * kd, 512),
                                         start=(nmm == 0), stop=(nmm == 23)),
                                reads=(f"w1h{slot}", ntn), writes=(bank,) if nmm == 0 else (),
                                wnd=(bank,) if nmm else ())
                            nmm += 1
                gg = sm(S_T1, 512)
                mt = sm(S_T2, 512)
                P.s(lambda E, gg=gg, jcp=jcp, slot=slot:
                    E.activation(gg, pb[2 * slot + 1][:, 0:512], AF.Gelu, bias=sm(S_BIAS + 24 + 2 * jcp + 1, 1)),
                    reads=(PBN[2 * slot + 1], "b1c"), writes=("sqA",))
                P.v(lambda E, jcp=jcp, gg=gg, mt=mt, slot=slot:
                    E.scalar_tensor_tensor(mt, pb[2 * slot][:, 0:512], sm(S_BIAS + 24 + 2 * jcp, 1),
                                           gg, ALU.add, ALU.mult),
                    reads=(PBN[2 * slot], "sqA", "b1c"), writes=("sqB",))
                P.s(lambda E, jcp=jcp, mt=mt: E.copy(shb(F_MB, 512 * jcp, 512), mt),
                    reads=("sqB",), writes=("mb",) if jcp == 0 else (), wnd=("mb",))
                P.v(lambda E, jcp=jcp, mt=mt:
                    E.tensor_tensor(shb(F_MB + 5632, 512 * jcp, 512), mt,
                                    shb(F_MB, 512 * jcp, 512), ALU.subtract),
                    reads=("sqB", "mb"), wnd=("mb",))
                if jcp + 2 < NJC:
                    w1_load(jcp + 2)
                elif jcp == NJC - 2:
                    w2_load(0)
                else:
                    w2_load(1)

            if tb < 3:
                emit_ln(tb + 1)
            for jc in range(NJC):
                slot = jc % 2
                for dc in range(NDT):
                    bank = PBN[dc]
                    for pi, (wd, xd) in enumerate(((0, 0), (0, 11264), (1024, 0))):
                        P.t(lambda E, jc=jc, slot=slot, dc=dc, wd=wd, xd=xd, pi=pi:
                            E.matmul(pb[dc][:, 0:512],
                                     shb(F_W2H + 1024 * slot, wd + 128 * dc, 128),
                                     shb(F_MB, xd + 512 * jc, 512),
                                     start=(jc == 0 and pi == 0), stop=(jc == NJC - 1 and pi == 2)),
                            reads=(f"w2h{slot}", "mb"), writes=(bank,) if (jc == 0 and pi == 0) else (),
                            wnd=(bank,) if not (jc == 0 and pi == 0) else ())
                if jc + 2 < NJC:
                    w2_load(jc + 2)
                elif tb < 3 and jc == NJC - 2:
                    w1_load(0)
                elif tb < 3 and jc == NJC - 1:
                    w1_load(1)
            for dc in range(NDT):
                bank = PBN[dc]
                P.v(lambda E, dc=dc, cu=cu:
                    E.scalar_tensor_tensor(az(dc, cu, 512), pb[dc][:, 0:512],
                                           sm(S_BIAS + 84 + dc, 1),
                                           az(dc, cu, 512), ALU.add, ALU.add),
                    reads=(bank, "b2c", AZB[dc]), writes=(AZB[dc],))

        if f"l{l}f" in dbg_out:
            P.dma(lambda E, l=l: E.dma_start(out=dbg_out[f"l{l}f"], in_=AZ[:]), reads=AZB,
                  writes=(f"dbgl{l}f",))

    P.fence(ATTN_BUFS + FFN_BUFS + RWKV_BUFS + ("wq", "wk"), ("x1r", "ob0", "ob1"),
            sm(S_MISC + 170, 1))
    out_toks = []
    if do_final:
        if do_rwkv:
            P.dma(lambda E: E.dma_start(out=sh(0, NDT * COL), in_=x1sp),
                  reads=("x1sp",), writes=("x1r",), extra=P.deps_of(*ATTN_BUFS, *FFN_BUFS))
            P.v(lambda E: E.tensor_tensor(AZ[:], AZ[:], sh(0, NDT * COL), ALU.add),
                reads=("x1r",) + AZB, writes=AZB)
        for j in range(16):
            slot = j % 2
            ob = sh(F_NT + 1024 * slot, 1024)
            obn = f"ob{slot}"
            for g4 in range(2):
                bank = PBN[g4]
                for q in range(4):
                    P.t(lambda E, j=j, g4=g4, q=q:
                        E.transpose(pb[g4][:, 128 * q:128 * (q + 1)],
                                    az(g4 * 4 + q, 1 + 128 * j, 128), ID[:]),
                        reads=(AZB[g4 * 4 + q], "ID"), writes=(bank,) if q == 0 else (),
                        wnd=(bank,) if q else ())
                P.s(lambda E, ob=ob, g4=g4:
                    E.copy(ob[:, 512 * g4:512 * (g4 + 1)], pb[g4][:, 0:512]),
                    reads=(bank,), writes=(obn,) if g4 == 0 else (), wnd=(obn,))
            out_toks.append(
                P.dma(lambda E, j=j, ob=ob: E.dma_start(out=out_h[128 * j:128 * (j + 1), :], in_=ob),
                      reads=(obn,), writes=(f"outd{j}",)))
    else:
        for j in range(16):
            slot = j % 2
            ob = sh(F_NT + 1024 * slot, 1024)
            obn = f"ob{slot}"
            for g4 in range(2):
                bank = PBN[g4]
                for q in range(4):
                    P.t(lambda E, j=j, g4=g4, q=q:
                        E.transpose(pb[g4][:, 128 * q:128 * (q + 1)],
                                    az(g4 * 4 + q, 1 + 128 * j, 128), ID[:]),
                        reads=(AZB[g4 * 4 + q], "ID"), writes=(bank,) if q == 0 else (),
                        wnd=(bank,) if q else ())
                P.s(lambda E, ob=ob, g4=g4:
                    E.copy(ob[:, 512 * g4:512 * (g4 + 1)], pb[g4][:, 0:512]),
                    reads=(bank,), writes=(obn,) if g4 == 0 else (), wnd=(obn,))
            out_toks.append(
                P.dma(lambda E, j=j, ob=ob: E.dma_start(out=out_h[128 * j:128 * (j + 1), :], in_=ob),
                      reads=(obn,), writes=(f"outd{j}",)))

    for name in dbg:
        out_toks.extend(P.deps_of(f"dbg{name}"))
    P.final_wait("sync", out_toks)
    P.emit()
    es.close()
    return nc



def _bf16():
    import ml_dtypes
    return ml_dtypes.bfloat16


def _dh_interleave():
    dh = np.empty(64, np.int64)
    dh[0::2] = np.arange(32)
    dh[1::2] = np.arange(32) + 32
    return dh


def _host_constants():
    dh = _dh_interleave()
    t = np.arange(NT, dtype=np.float64)
    half = np.arange(0, 64, 2, dtype=np.float64)
    inv_freq = 1.0 / 10000.0 ** (half / 64.0)
    fh = t[:, None] * inv_freq[None, :]
    freqs = np.concatenate([fh, fh], -1)
    base_scale = (half + 0.4 * 64) / (1.4 * 64)
    power = (t - NT // 2) / (WIN // 2)
    sc_h = base_scale[None, :] ** power[:, None]
    scale = np.concatenate([sc_h, sc_h], -1)
    cos, sin = np.cos(freqs), np.sin(freqs)
    p = np.arange(128)
    dhp = dh[p % 64]
    sign = np.where((p % 64) % 2 == 0, -1.0, 1.0)
    c1q = (cos[:, dhp] * scale[:, dhp]).T
    c2q = (sign[:, None] * (sin[:, dhp] * scale[:, dhp]).T)
    inv = 1.0 / scale
    c1k = (cos[:, dhp] * inv[:, dhp]).T
    c2k = (sign[:, None] * (sin[:, dhp] * inv[:, dhp]).T)
    e8 = np.zeros((128, 4, 8), np.float32)
    r8 = np.zeros((8, 4, 128), np.float32)
    for i in range(4):
        for pp in range(128):
            c = 2 * i + (1 if pp >= 64 else 0)
            e8[pp, i, c] = 1.0
            r8[c, i, pp] = 1.0
    ind2 = np.zeros((1, 2, 128), np.float32)
    ind2[0, 0, 0:64] = 1.0
    ind2[0, 1, 64:128] = 1.0
    return {
        "c1q": np.ascontiguousarray(c1q, np.float32),
        "c2q": np.ascontiguousarray(c2q, np.float32),
        "c1k": np.ascontiguousarray(c1k, np.float32),
        "c2k": np.ascontiguousarray(c2k, np.float32),
        "e8": e8, "r8": r8, "ind2": ind2,
        "onesr": np.ones((1, 128), np.float32),
        "onesc": np.ones((128, 1), np.float32),
        "ident": np.eye(128, dtype=np.float32),
    }


def _host_weights(ins):
    dh = _dh_interleave()
    qperm = np.empty(D, np.int64)
    for i in range(8):
        for pp in range(128):
            head = 2 * i + (1 if pp >= 64 else 0)
            qperm[i * 128 + pp] = head * 64 + dh[pp % 64]
    wqkv = np.asarray(ins["L0_ffn_Wqkv"])
    wqkvR = np.concatenate([wqkv[:, :, qperm], wqkv[:, :, 1024 + qperm], wqkv[:, :, 2048:]], axis=2)
    woperm = []
    for hg in range(2):
        for (a, b) in ((0, 2), (4, 6), (1, 3), (5, 7)):
            woperm.extend(range((8 * hg + a) * 64, (8 * hg + a) * 64 + 64))
            woperm.extend(range((8 * hg + b) * 64, (8 * hg + b) * 64 + 64))
    woperm = np.array(woperm)
    woR = np.asarray(ins["L0_ffn_Wo"])[:, woperm, :]
    w1 = np.asarray(ins["L0_ffn_W1"])
    b1 = np.asarray(ins["L0_ffn_b1"])
    w2 = np.asarray(ins["L0_ffn_W2"])
    w1p = np.zeros((NL, D, 2 * DI), np.float32)
    b1p = np.zeros((NL, 128, 2 * NJC), np.float32)
    w2p = np.zeros((NL, DI, D), np.float32)
    w2p[:, :2730, :] = w2
    for i in range(NJC):
        a0, a1 = 128 * i, min(128 * (i + 1), 2730)
        n = a1 - a0
        w1p[:, :, 256 * i:256 * i + n] = w1[:, :, a0:a1]
        w1p[:, :, 256 * i + 128:256 * i + 128 + n] = w1[:, :, 2730 + a0:2730 + a1]
        b1p[:, :n, 2 * i] = b1[:, a0:a1]
        b1p[:, :n, 2 * i + 1] = b1[:, 2730 + a0:2730 + a1]
    qsc = np.asarray(ins["L0_ffn_q_scale"])
    ksc = np.asarray(ins["L0_ffn_k_scale"])
    p = np.arange(128)
    dhp = dh[p % 64]
    qs = qsc[:, dhp][:, :, None]
    ks = ksc[:, dhp][:, :, None]
    dhs = dh[(p % 64) ^ 1]
    qss = qsc[:, dhs][:, :, None]
    kss = ksc[:, dhs][:, :, None]
    return {
        "wqkv": np.ascontiguousarray(wqkvR, np.float32),
        "wo": np.ascontiguousarray(woR, np.float32),
        "w1": w1p, "w2": w2p, "b1c": b1p,
        "w1b": w1p.view(_bf16()), "w2b": w2p.view(_bf16()),
        "wob": np.ascontiguousarray(woR, np.float32).view(_bf16()),
        "lng": np.ascontiguousarray(np.asarray(ins["L0_ffn_ln_g"]).reshape(NL, NDT, 128).transpose(0, 2, 1)),
        "lnbf": np.ascontiguousarray(np.asarray(ins["L0_ffn_ln_b"]).reshape(NL, NDT, 128).transpose(0, 2, 1)),
        "b2": np.ascontiguousarray(np.asarray(ins["L0_ffn_b2"]).reshape(NL, NDT, 128).transpose(0, 2, 1)),
        "qs": np.ascontiguousarray(qs, np.float32),
        "qss": np.ascontiguousarray(qss, np.float32),
        "ks": np.ascontiguousarray(ks, np.float32),
        "kss": np.ascontiguousarray(kss, np.float32),
        "qsrow": np.ascontiguousarray(qsc[:, None, :], np.float32),
        "ksrow": np.ascontiguousarray(ksc[:, None, :], np.float32),
        "kvwt": np.ascontiguousarray(
            np.concatenate([np.asarray(ins["L0_attn_key_w"]).T,
                            np.asarray(ins["L0_attn_value_w"]).T], axis=1), np.float32),
        "owt": np.ascontiguousarray(np.asarray(ins["L0_attn_output_w"]).T, np.float32),
        "td": np.ascontiguousarray(np.asarray(ins["L0_attn_time_decay"])[:, None], np.float32),
        "tf": np.ascontiguousarray(np.asarray(ins["L0_attn_time_first"])[:, None], np.float32),
        "gatesf": np.ascontiguousarray(
            np.asarray(ins["L0_attn_time_shift_gates"]).reshape(NDT, 128).T, np.float32),
        "lnw": np.ascontiguousarray(np.asarray(ins["L0_attn_ln_w"]).reshape(NDT, 128).T, np.float32),
        "lnb": np.ascontiguousarray(np.asarray(ins["L0_attn_ln_b"]).reshape(NDT, 128).T, np.float32),
    }


def kernel(**inputs):
    x = np.asarray(inputs["L0_attn_x"])
    B = x.shape[0]
    consts = _host_constants()
    weights = _host_weights(inputs)
    nc = build_program()
    in_maps = []
    for b in range(B):
        m = {"x": np.ascontiguousarray(x[b], np.float32)}
        m.update(consts)
        m.update(weights)
        in_maps.append(m)
    res = run_bass_kernel_spmd(nc, in_maps, core_ids=list(range(B)))
    return np.stack([np.asarray(r["out"]) for r in res.results], axis=0)


# version 15 (final)
# speedup vs baseline: 1.0803x; 1.0803x over previous
import numpy as np
from contextlib import ExitStack

import concourse.bass as bass
import concourse.mybir as mybir
from concourse.bass_utils import run_bass_kernel_spmd

F32 = mybir.dt.float32
F32R = mybir.dt.float32r
AF = mybir.ActivationFunctionType
ALU = mybir.AluOpType
AX = mybir.AxisListType

NT = 2048
D = 1024
NDT = 8
COL = 2060
WIN = 256
NW = 8
NL = 6
DI = 2816
NJC = 22
NEG = -60000.0
SHUF = [x for i in range(16) for x in (2 * i + 1, 2 * i)]
ENGINES = ("sync", "tensor", "vector", "scalar", "gpsimd")


class PB:

    def __init__(self, nc, es):
        self.nc = nc
        self.es = es
        self.steps = {e: [] for e in ENGINES}
        self.sems = {}
        self.cnt = {}
        self.last_w = {}
        self.readers = {}
        self.dma_rr = {}
        self.preludes = {}

    def _sem(self, eng, dma):
        if dma:
            pool = 10 if eng == "sync" else 2
            idx = self.dma_rr.get(eng, 0)
            self.dma_rr[eng] = idx + 1
            name = "d" + eng[0].upper() + str(idx % pool)
        else:
            name = "c" + eng[0].upper()
        if name not in self.sems:
            self.sems[name] = self.es.enter_context(self.nc.semaphore(name))
            self.cnt[name] = 0
        return name

    def deps_of(self, *bufs):
        out = []
        for b in bufs:
            t = self.last_w.get(b)
            if t:
                out.append(t)
            out.extend(self.readers.get(b, ()))
        return out

    def op(self, eng, fn, reads=(), writes=(), wnd=(), extra=(), dma=False):
        sname = self._sem(eng, dma)
        deps = list(extra)
        if dma and self.cnt[sname] > 0:
            deps.append((sname, self.cnt[sname]))
        for b in reads:
            t = self.last_w.get(b)
            if t:
                deps.append(t)
        for b in writes:
            t = self.last_w.get(b)
            if t:
                deps.append(t)
            deps.extend(self.readers.get(b, ()))
        waits, drain = {}, False
        for (s, v) in deps:
            if waits.get(s, -1) < v:
                waits[s] = v
        inc = 16 if dma else 1
        self.cnt[sname] += inc
        tok = (sname, self.cnt[sname])
        self.steps[eng].append((fn, tuple(sorted(waits.items())), drain, sname, inc))
        for b in reads:
            self.readers.setdefault(b, []).append(tok)
        for b in writes:
            self.last_w[b] = tok
            self.readers[b] = []
        for b in wnd:
            self.last_w[b] = tok
            self.readers[b] = []
        return tok

    def t(self, fn, **kw):
        return self.op("tensor", fn, **kw)

    def v(self, fn, **kw):
        return self.op("vector", fn, **kw)

    def s(self, fn, **kw):
        return self.op("scalar", fn, **kw)

    def g(self, fn, **kw):
        return self.op("gpsimd", fn, **kw)

    def dma(self, fn, eng="sync", **kw):
        return self.op(eng, fn, dma=True, **kw)

    def fence(self, old_bufs, new_bufs, smslice):
        deps = self.deps_of(*old_bufs)
        tok = self.op("vector", lambda E: E.memset(smslice, 0.0), extra=deps)
        for b in new_bufs:
            self.last_w[b] = tok
            self.readers[b] = []

    def pe_barrier(self):
        self.steps["tensor"].append((None, (), True, None, 0))

    def final_wait(self, eng, toks):
        wd = {}
        for (s, v) in toks:
            if wd.get(s, -1) < v:
                wd[s] = v
        self.steps[eng].append((None, tuple(sorted(wd.items())), False, None, 0))

    def emit(self):
        block = self.es.enter_context(self.nc.Block())
        for eng in ENGINES:
            steps = self.steps[eng]
            if not steps:
                continue
            section = getattr(block, eng)

            def body(E, steps=steps, eng=eng):
                pre = self.preludes.get(eng)
                if pre:
                    pre(E)
                seen = {}
                for fn, waits, drain, sname, inc in steps:
                    if drain:
                        E.drain()
                    for (s, v) in waits:
                        if seen.get(s, -1) < v:
                            E.wait_ge(self.sems[s], v)
                            seen[s] = v
                    if fn is None:
                        continue
                    fn(E).then_inc(self.sems[sname], inc)
            section(body)


def kcs_of(w):
    return (2, 3) if w == 0 else (0, 1, 2, 3)


def build_program(n_layers=NL, do_rwkv=True, do_final=True, dbg=(), qkv_bf16=True):
    nc = bass.Bass(trn_type="TRN2", target_bir_lowering=False)
    es = ExitStack()

    def din(name, shape):
        return nc.dram_tensor(name, shape, F32, kind="ExternalInput").ap()

    x_h = din("x", [NT, D])
    out_h = nc.dram_tensor("out", [NT, D], F32, kind="ExternalOutput").ap()
    x1sp = nc.dram_tensor("x1sp", [128, NDT * COL], F32, kind="Internal").ap()
    wosp = nc.dram_tensor("wosp", [128, NW * 2048], F32, kind="Internal").ap()

    wqkv_h = din("wqkv", [NL, D, 3 * D])
    wo_h = din("wo", [NL, D, D])
    w1_h = din("w1", [NL, D, 2 * DI])
    w1b_h = nc.dram_tensor("w1b", [NL, D, 4 * DI], mybir.dt.bfloat16, kind="ExternalInput").ap()
    w2b_h = nc.dram_tensor("w2b", [NL, DI, 2 * D], mybir.dt.bfloat16, kind="ExternalInput").ap()
    wob_h = nc.dram_tensor("wob", [NL, D, 2 * D], mybir.dt.bfloat16, kind="ExternalInput").ap()
    w2_h = din("w2", [NL, DI, D])
    b1c_h = din("b1c", [NL, 128, 2 * NJC])
    lng_h = din("lng", [NL, 128, NDT])
    lnbf_h = din("lnbf", [NL, 128, NDT])
    b2_h = din("b2", [NL, 128, NDT])
    qs_h = din("qs", [NL, 128, 1])
    qss_h = din("qss", [NL, 128, 1])
    ks_h = din("ks", [NL, 128, 1])
    kss_h = din("kss", [NL, 128, 1])
    qsrow_h = din("qsrow", [NL, 1, 64])
    ksrow_h = din("ksrow", [NL, 1, 64])
    c1q_h = din("c1q", [128, NT])
    c2q_h = din("c2q", [128, NT])
    c1k_h = din("c1k", [128, NT])
    c2k_h = din("c2k", [128, NT])
    e8_h = din("e8", [128, 4, 8])
    r8_h = din("r8", [8, 4, 128])
    ind2_h = din("ind2", [1, 2, 128])
    onesr_h = din("onesr", [1, 128])
    onesc_h = din("onesc", [128, 1])
    ident_h = din("ident", [128, 128])
    kvwt_h = din("kvwt", [D, 128])
    owt_h = din("owt", [64, D])
    td_h = din("td", [64, 1])
    tf_h = din("tf", [64, 1])
    gates_h = din("gatesf", [128, NDT])
    lnw_h = din("lnw", [128, NDT])
    lnb_h = din("lnb", [128, NDT])

    dbg_out = {}
    for name in dbg:
        shp = [128, NDT * COL] if name[0] in "xzl" else [128, 8192]
        dbg_out[name] = nc.dram_tensor("dbg_" + name, shp, F32,
                                       kind="ExternalOutput").ap()

    AZ = es.enter_context(nc.sbuf_tensor("AZ", [128, NDT * COL], F32))
    SH = es.enter_context(nc.sbuf_tensor("SH", [128, 30752], F32))
    SM = es.enter_context(nc.sbuf_tensor("SM", [128, 4352], F32))
    ID = es.enter_context(nc.sbuf_tensor("ID", [128, 128], F32))
    E8 = es.enter_context(nc.sbuf_tensor("E8", [128, 32], F32))
    R8 = es.enter_context(nc.sbuf_tensor("R8", [128, 512], F32))
    ONESC = es.enter_context(nc.sbuf_tensor("ONESC", [128, 1], F32))
    pb = [es.enter_context(nc.psum_tensor(f"pb{i}", [128, 512], F32)) for i in range(8)]
    PBN = tuple(f"pb{i}" for i in range(8))

    def az(dt_, c0, n):
        return AZ[:, dt_ * COL + c0: dt_ * COL + c0 + n]

    def sh(off, n, p0=0, p1=128):
        return SH[p0:p1, off: off + n]

    SHB = SH[:].bitcast(mybir.dt.bfloat16)

    def shb(base_w, eoff, n, p0=0, p1=128):
        return SHB[p0:p1, 2 * base_w + eoff: 2 * base_w + eoff + n]

    def sm(off, n, p0=0, p1=128):
        return SM[p0:p1, off: off + n]

    ONESR = SM[0:1, 3584 + 96: 3584 + 224]
    INDLO = SM[0:1, 3584 + 224: 3584 + 352]
    INDHI = SM[0:1, 3584 + 352: 3584 + 480]

    O_WQ, O_WK, O_WV, O_WO = 0, 4096, 8192, 12288
    O_KR, O_VR, O_QS, O_SS, O_OS = 16384, 18432, 20512, 21536, 25632
    O_WOH = 26656
    F_MB, F_NT = 0, 11264
    F_W1S, F_W1H, F_W2S, F_W2H = 19456, 23552, 27648, 28672
    R_Y, R_K, R_V, R_WKV, R_ST, R_C = 0, 16384, 18432, 20480, 22528, 24576
    S_T1, S_T2, S_TAB = 0, 512, 1024
    S_STA, S_STB, S_STC = 2048, 2560, 3072
    S_BIAS, S_MISC = 3584, 4096

    regs = {}

    def gps_prelude(E):
        regs["neg"] = E.alloc_register("negfill")
        E.reg_mov(regs["neg"], NEG)

    ATTN_BUFS = ("wq", "wk", "wv", "wo", "kr0", "kr1", "vr0", "vr1", "vr2", "vr3",
                 "qsb", "ss", "zsp0", "zsp1", "woh", "wol", "os0", "os1", "os2", "os3",
                 "tA1q", "tA2q", "tA1k", "tA2k")
    FFN_BUFS = ("mb", "ntq0", "ntq1", "w1b0", "w1b1", "w2b0", "w2c1", "w1h0", "w1h1", "w2h0", "w2h1")
    RWKV_BUFS = ("y", "rk", "rv", "rwkvb", "rst", "rwc")
    AZB = tuple(f"az{d}" for d in range(NDT))

    P = PB(nc, es)
    P.preludes["gpsimd"] = gps_prelude

    P.dma(lambda E: E.dma_start(out=ID[:], in_=ident_h), writes=("ID",))
    P.dma(lambda E: E.dma_start(out=E8[:, 0:32].rearrange("p (a b) -> p a b", a=4),
                                in_=e8_h), writes=("E8",))
    P.dma(lambda E: E.dma_start(out=R8[0:8, 0:512].rearrange("p (a b) -> p a b", a=4),
                                in_=r8_h), writes=("R8",))
    P.dma(lambda E: E.dma_start(out=INDLO, in_=ind2_h[:, 0, :]), writes=("IND",))
    P.dma(lambda E: E.dma_start(out=INDHI, in_=ind2_h[:, 1, :]), writes=("IND",))
    P.dma(lambda E: E.dma_start(out=ONESR, in_=onesr_h), writes=("ONESR",))
    P.dma(lambda E: E.dma_start(out=ONESC[:], in_=onesc_h), writes=("ONESC",))

    P.v(lambda E: E.memset(AZ[:], 0.0), writes=AZB)
    for j in range(16):
        slot = j % 2
        xa = sh(slot * 1024, 1024)
        xb = ("wq", "wk")[slot]
        P.dma(lambda E, j=j, xa=xa: E.dma_start(out=xa, in_=x_h[128 * j:128 * (j + 1), :]),
              writes=(xb,))
        for g4 in range(2):
            bank = PBN[g4]
            for q in range(4):
                P.t(lambda E, xa=xa, dt_=g4 * 4 + q, q=q, g4=g4:
                    E.transpose(pb[g4][:, 128 * q:128 * (q + 1)],
                                xa[:, 128 * dt_:128 * (dt_ + 1)], ID[:]),
                    reads=(xb, "ID"), writes=(bank,) if q == 0 else (), wnd=(bank,) if q else ())
            dst = AZ[:].rearrange("p (d c) -> p d c", d=NDT)[:, g4 * 4:(g4 + 1) * 4,
                                                            1 + 128 * j: 129 + 128 * j]
            P.s(lambda E, dst=dst, g4=g4:
                E.copy(dst, pb[g4][:].rearrange("p (a b) -> p a b", a=4)),
                reads=(bank,), writes=(AZB[g4 * 4], AZB[g4 * 4 + 1], AZB[g4 * 4 + 2], AZB[g4 * 4 + 3]))

    def ln_block(read_slice, n_cols, write_fn, affine=None, eps_only_rstd=False, tag=""):
        srcs = [read_slice(dt_) for dt_ in range(NDT)]
        for dt_ in range(NDT):
            ap_, b_ = srcs[dt_]
            P.t(lambda E, ap_=ap_, dt_=dt_, n=n_cols:
                E.matmul(pb[4][0:1, 0:n], ONESC[:], ap_, start=(dt_ == 0), stop=(dt_ == NDT - 1)),
                reads=(b_, "ONESC"), writes=("pb4",) if dt_ == 0 else (), wnd=("pb4",) if dt_ else ())
            sqb = ("sqA", "sqB")[dt_ % 2]
            sqa = sm((S_T1, S_T2)[dt_ % 2], n_cols)
            P.s(lambda E, ap_=ap_, sqa=sqa: E.activation(sqa, ap_, AF.Square),
                reads=(b_,), writes=(sqb,))
            P.t(lambda E, sqa=sqa, dt_=dt_, n=n_cols:
                E.matmul(pb[5][0:1, 0:n], ONESC[:], sqa, start=(dt_ == 0), stop=(dt_ == NDT - 1)),
                reads=(sqb, "ONESC"), writes=("pb5",) if dt_ == 0 else (), wnd=("pb5",) if dt_ else ())
        mu = sm(S_STA, n_cols, 0, 1)
        v1 = sm(S_STB, n_cols, 0, 1)
        v2 = sm(S_STC, n_cols, 0, 1)
        sd = v2
        rstd = v1
        P.v(lambda E: E.tensor_single_scalar(mu, pb[4][0:1, 0:n_cols], 1.0 / D, ALU.mult),
            reads=("pb4",), writes=("stA",))
        P.v(lambda E: E.tensor_single_scalar(v1, pb[5][0:1, 0:n_cols], 1.0 / D, ALU.mult),
            reads=("pb5",), writes=("stB",))
        P.v(lambda E: E.tensor_tensor(v2, mu, mu, ALU.mult), reads=("stA",), writes=("stC",))
        P.v(lambda E: E.tensor_tensor(v1, v1, v2, ALU.subtract), reads=("stC",), writes=("stB",))
        P.v(lambda E: E.tensor_single_scalar(v1, v1, 1e-5, ALU.add), reads=(), writes=("stB",))
        P.s(lambda E: E.activation(sd, v1, AF.Sqrt), reads=("stB",), writes=("stC",))
        P.v(lambda E: E.reciprocal(rstd, sd), reads=("stC",), writes=("stB",))
        P.t(lambda E, n=n_cols: E.matmul(pb[6][:, 0:n], ONESR, mu, start=True, stop=True),
            reads=("stA", "ONESR"), writes=("pb6",))
        P.t(lambda E, n=n_cols: E.matmul(pb[7][:, 0:n], ONESR, rstd, start=True, stop=True),
            reads=("stB", "ONESR"), writes=("pb7",))
        for dt_ in range(NDT):
            write_fn(dt_, srcs[dt_])

    if do_rwkv:
        rc = R_C
        P.dma(lambda E: E.dma_start(out=sh(rc, 1024).rearrange("p (k m) -> p k m", k=8),
                                    in_=kvwt_h.rearrange("(k p) m -> p k m", p=128)),
              writes=("rwc",), extra=P.deps_of("wq", "wk"))
        P.dma(lambda E: E.dma_start(out=sh(rc + 1024, 1024, 0, 64), in_=owt_h), wnd=("rwc2",))
        for nm, hbm, off in (("gts", gates_h, 0), ("lnw", lnw_h, 8), ("lnb", lnb_h, 16)):
            P.dma(lambda E, hbm=hbm, off=off: E.dma_start(out=sm(S_BIAS + off, 8), in_=hbm),
                  writes=(nm,))
        P.dma(lambda E: E.dma_start(out=sm(S_MISC + 16, 1, 0, 64), in_=tf_h), writes=("tf",))
        P.dma(lambda E: E.dma_start(out=sm(S_MISC + 17, 1, 0, 64), in_=td_h), writes=("td",))
        gcol = sm(S_MISC + 0, 8)
        omg = sm(S_MISC + 8, 8)
        P.s(lambda E: E.activation(gcol, sm(S_BIAS, 8), AF.Sigmoid), reads=("gts",), writes=("gcol",))
        P.v(lambda E: E.tensor_scalar(omg, gcol, -1.0, 1.0, ALU.mult, ALU.add),
            reads=("gcol",), writes=("omg",))
        tfe = sm(S_MISC + 18, 1, 0, 64)
        ntf = sm(S_MISC + 19, 1, 0, 64)
        wcol = sm(S_MISC + 20, 1, 0, 64)
        P.s(lambda E: E.activation(tfe, sm(S_MISC + 16, 1, 0, 64), AF.Exp), reads=("tf",), writes=("tfe",))
        P.v(lambda E: E.tensor_single_scalar(ntf, tfe, -1.0, ALU.mult), reads=("tfe",), writes=("ntf",))
        P.s(lambda E: E.activation(wcol, sm(S_MISC + 17, 1, 0, 64), AF.Exp), reads=("td",), writes=("wcol",))

        for blk in range(4):
            c0 = 1 + 512 * blk

            def wr(dt_, src, blk=blk, c0=c0):
                ap_, b_ = src
                ya = sh(R_Y + dt_ * 2048 + 512 * blk, 512)
                t1 = sm(S_T1, 512)
                P.v(lambda E, ap_=ap_, t1=t1: E.tensor_tensor(t1, ap_, pb[6][:, 0:512], ALU.subtract),
                    reads=(b_, "pb6"), writes=("sqA",))
                P.v(lambda E, ya=ya, t1=t1: E.tensor_tensor(ya, t1, pb[7][:, 0:512], ALU.mult),
                    reads=("sqA", "pb7"), writes=("y",), extra=P.deps_of("wq", "wk") if (dt_ == 0 and blk == 0) else ())
            ln_block(lambda dt_, c0=c0: (az(dt_, c0, 512), AZB[dt_]), 512, wr)

        for dt_ in range(NDT):
            gc = sm(S_MISC + 0 + dt_, 1)
            oc = sm(S_MISC + 8 + dt_, 1)
            for seg in (0, 512):
                lo = sh(R_Y + dt_ * 2048 + seg, 512)
                up = sh(R_Y + dt_ * 2048 + 1024 + seg, 512)
                t1 = sm(S_T1, 512)
                t2 = sm(S_T2, 512)
                P.v(lambda E, t1=t1, lo=lo: E.tensor_copy(t1, lo), reads=("y",), writes=("sqA",))
                P.v(lambda E, t2=t2, t1=t1, oc=oc: E.tensor_single_scalar(t2, t1, oc, ALU.mult),
                    reads=("sqA", "omg"), writes=("sqB",))
                P.v(lambda E, lo=lo, up=up, gc=gc, t2=t2:
                    E.scalar_tensor_tensor(lo, up, gc, t2, ALU.mult, ALU.add),
                    reads=("y", "sqB", "gcol"), writes=("y",))
                P.v(lambda E, t2=t2, up=up, oc=oc: E.tensor_single_scalar(t2, up, oc, ALU.mult),
                    reads=("y", "omg"), writes=("sqB",))
                P.v(lambda E, up=up, t1=t1, gc=gc, t2=t2:
                    E.scalar_tensor_tensor(up, t1, gc, t2, ALU.mult, ALU.add),
                    reads=("sqA", "sqB", "gcol"), writes=("y",))

        for blk in range(4):
            def wr2(dt_, src, blk=blk):
                ap_, b_ = src
                t1 = sm(S_T1, 512)
                gc = sm(S_BIAS + 8 + dt_, 1)
                bc = sm(S_BIAS + 16 + dt_, 1)
                P.v(lambda E, ap_=ap_, t1=t1: E.tensor_tensor(t1, ap_, pb[6][:, 0:512], ALU.subtract),
                    reads=(b_, "pb6"), writes=("sqA",))
                P.v(lambda E, ap_=ap_, t1=t1: E.tensor_tensor(t1, t1, pb[7][:, 0:512], ALU.mult),
                    reads=("sqA", "pb7"), writes=("sqA",))
                P.v(lambda E, ap_=ap_, t1=t1, gc=gc, bc=bc:
                    E.tensor_scalar(ap_, t1, gc, bc, ALU.mult, ALU.add),
                    reads=("sqA", "lnw", "lnb"), writes=("y",))
            ln_block(lambda dt_, blk=blk: (sh(R_Y + dt_ * 2048 + 512 * blk, 512), "y"), 512, wr2)

        for nb in range(4):
            for which, col0, dstoff, dstb in (("k", 0, R_K, "rk"), ("v", 64, R_V, "rv")):
                bank = PBN[nb % 2]
                for kd in range(NDT):
                    P.t(lambda E, kd=kd, nb=nb, col0=col0:
                        E.matmul(pb[nb % 2][0:64, 0:512],
                                 sh(rc + kd * 128 + col0, 64),
                                 sh(R_Y + kd * 2048 + 512 * nb, 512),
                                 start=(kd == 0), stop=(kd == NDT - 1)),
                        reads=("y", "rwc"), writes=(bank,) if kd == 0 else (), wnd=(bank,) if kd else ())
                P.s(lambda E, nb=nb, dstoff=dstoff:
                    E.copy(sh(dstoff + 512 * nb, 512, 0, 64), pb[nb % 2][0:64, 0:512]),
                    reads=(bank,), writes=(dstb,))
        P.s(lambda E: E.activation(sh(R_WKV, 2048, 0, 64), sh(R_K, 2048, 0, 64), AF.Exp,
                                   scale=ntf), reads=("rk", "ntf"), writes=("rwkvb",))
        P.v(lambda E: E.tensor_tensor(sh(R_WKV, 2048, 0, 64), sh(R_WKV, 2048, 0, 64),
                                      sh(R_V, 2048, 0, 64), ALU.mult),
            reads=("rwkvb", "rv"), writes=("rwkvb",))
        P.v(lambda E: E.tensor_tensor_scan(sh(R_ST, 2048, 0, 64),
                                           wcol.broadcast_to([64, 2048]),
                                           sh(R_WKV, 2048, 0, 64), 0.0,
                                           ALU.mult, ALU.add),
            reads=("rwkvb", "wcol"), writes=("rst",))
        for dc in range(NDT):
            for nb in range(4):
                bank = PBN[(dc * 4 + nb) % 2]
                P.t(lambda E, dc=dc, nb=nb:
                    E.matmul(pb[(dc * 4 + nb) % 2][:, 0:512],
                             sh(rc + 1024 + 128 * dc, 128, 0, 64),
                             sh(R_ST + 512 * nb, 512, 0, 64), start=True, stop=True),
                    reads=("rst", "rwc2"), writes=(bank,))
                P.v(lambda E, dc=dc, nb=nb:
                    E.scalar_tensor_tensor(az(dc, 1 + 512 * nb, 512),
                                           pb[(dc * 4 + nb) % 2][:, 0:512], 1.0,
                                           az(dc, 1 + 512 * nb, 512), ALU.mult, ALU.add),
                    reads=(bank, AZB[dc]), writes=(AZB[dc],))

        spill_tok = P.dma(lambda E: E.dma_start(out=x1sp, in_=AZ[:]), reads=AZB, writes=("x1sp",))

        if "x1" in dbg_out:
            P.dma(lambda E: E.dma_start(out=dbg_out["x1"], in_=AZ[:]), reads=AZB, writes=("dbgx1",))

        for blk in range(4):
            def wr3(dt_, src, blk=blk):
                ap_, b_ = src
                t1 = sm(S_T1, 512)
                P.v(lambda E, ap_=ap_, t1=t1: E.tensor_tensor(t1, ap_, pb[6][:, 0:512], ALU.subtract),
                    reads=(b_, "pb6"), writes=("sqA",))
                P.v(lambda E, ap_=ap_, t1=t1: E.tensor_tensor(ap_, t1, pb[7][:, 0:512], ALU.mult),
                    reads=("sqA", "pb7", "x1sp"), writes=(AZB[dt_],))
            ln_block(lambda dt_, blk=blk: (az(dt_, 1 + 512 * blk, 512), AZB[dt_]), 512, wr3)

    if "z0" in dbg_out:
        P.dma(lambda E: E.dma_start(out=dbg_out["z0"], in_=AZ[:]), reads=AZB, writes=("dbgz0",))

    for l in range(n_layers):
        P.fence(RWKV_BUFS + ("rwc2",) + FFN_BUFS + ("wq", "wk"), ATTN_BUFS, sm(S_MISC + 168, 1))
        lc = []
        for nm, hbm, off, wid in (("qscol", qs_h, 0, 1), ("qsscol", qss_h, 1, 1),
                                  ("kscol", ks_h, 2, 1), ("ksscol", kss_h, 3, 1)):
            P.dma(lambda E, hbm=hbm, off=off, l=l: E.dma_start(out=sm(S_MISC + 32 + off, 1), in_=hbm[l]),
                  writes=(nm,))
        P.dma(lambda E, l=l: E.dma_start(out=sm(S_MISC + 36, 64, 0, 1), in_=qsrow_h[l]), writes=("qsrow",))
        P.dma(lambda E, l=l: E.dma_start(out=sm(S_MISC + 100, 64, 0, 1), in_=ksrow_h[l]), writes=("ksrow",))
        P.dma(lambda E, l=l: E.dma_start(out=sm(S_BIAS + 24, 2 * NJC), in_=b1c_h[l]), writes=("b1c",))
        P.dma(lambda E, l=l: E.dma_start(out=sm(S_BIAS + 68, NDT), in_=lng_h[l]), writes=("lngc",))
        P.dma(lambda E, l=l: E.dma_start(out=sm(S_BIAS + 76, NDT), in_=lnbf_h[l]), writes=("lnbfc",))
        P.dma(lambda E, l=l: E.dma_start(out=sm(S_BIAS + 84, NDT), in_=b2_h[l]), writes=("b2c",))
        m1 = sm(S_MISC + 164, 1, 0, 1)
        m2 = sm(S_MISC + 165, 1, 0, 1)
        P.v(lambda E: E.reduce_max(m1, sm(S_MISC + 36, 64, 0, 1), AX.X, apply_absolute_value=True),
            reads=("qsrow",), writes=("m1",))
        P.v(lambda E: E.reduce_max(m2, sm(S_MISC + 100, 64, 0, 1), AX.X, apply_absolute_value=True),
            reads=("ksrow",), writes=("m2",))
        P.v(lambda E: E.tensor_tensor(m1, m1, m2, ALU.mult), reads=("m2",), writes=("m1",))
        P.v(lambda E: E.tensor_single_scalar(m1, m1, -8.0, ALU.mult), reads=(), writes=("m1",))
        P.t(lambda E: E.matmul(pb[2][:, 0:1], ONESR, m1, start=True, stop=True),
            reads=("m1", "ONESR"), writes=("pb2",))
        negc = sm(S_MISC + 166, 1)
        P.s(lambda E: E.activation(negc, pb[2][:, 0:1], AF.Copy), reads=("pb2",), writes=("negc",))

        for hg in range(2):
            for wname, base, creg in (("wq", O_WQ, 0), ("wk", O_WK, 1024), ("wv", O_WV, 2048)):
                for kd in range(NDT):
                    scr = sm((S_T1, S_T2, S_STA, S_STB)[kd % 4], 512)
                    scrb = ("sqA", "sqB", "stA", "stB")[kd % 4]
                    P.dma(lambda E, kd=kd, creg=creg, hg=hg, l=l, scr=scr:
                          E.dma_start(out=scr,
                                      in_=wqkv_h[l, 128 * kd:128 * kd + 128,
                                                 creg + 512 * hg:creg + 512 * hg + 512]),
                          writes=(scrb,))
                    P.s(lambda E, kd=kd, base=base, scr=scr:
                        E.copy(shb(base, 512 * kd, 512), scr),
                        reads=(scrb,), writes=(wname,) if kd == 0 else (), wnd=(wname,))
                    P.v(lambda E, kd=kd, base=base, scr=scr:
                        E.tensor_tensor(shb(base, 4096 + 512 * kd, 512), scr,
                                        shb(base, 512 * kd, 512), ALU.subtract),
                        reads=(scrb, wname), wnd=(wname,))
            P.dma(lambda E, hg=hg, l=l: E.dma_start(
                out=sh(O_WO, 4096).rearrange("p (k m) -> p k m", k=4),
                in_=wo_h[l, 512 * hg:512 * hg + 512, :].rearrange("(k p) m -> p k m", p=128)),
                writes=("wo",), extra=P.deps_of("zsp0", "zsp1"))
            P.s(lambda E: E.copy(shb(O_WOH, 0, 4096), sh(O_WO, 4096)),
                reads=("wo",), writes=("woh",))
            P.v(lambda E: E.tensor_tensor(shb(O_WOH + 2048, 0, 4096), sh(O_WO, 4096),
                                          shb(O_WOH, 0, 4096), ALU.subtract),
                reads=("wo", "woh"), writes=("wol",))
            P.fence(("wo",), ("zsp0", "zsp1"), sm(S_MISC + 173, 1))
            vr_ones = SH[:, O_VR:O_VR + 2080].rearrange("p (s h c) -> p s h c", s=4, h=8)[:, :, :, 64:65]
            P.v(lambda E: E.memset(vr_ones, 1.0),
                writes=("vr0", "vr1", "vr2", "vr3"))

            def zs_slice(dt_, c0, n):
                off = 0 if dt_ < 4 else -1
                return az(dt_, c0 + off, n)

            def emit_zsp(wn):
                slot = wn % 2
                zb = O_WO + 2048 * slot
                cuz = 1 + WIN * wn
                zn = f"zsp{slot}"
                for kd in range(NDT):
                    P.s(lambda E, kd=kd, cuz=cuz, zb=zb:
                        E.copy(shb(zb, 256 * kd, 256), zs_slice(kd, cuz, 256)),
                        reads=(AZB[kd],) + (("wo",) if kd == 0 else ()),
                        writes=(zn,) if kd == 0 else (), wnd=(zn,))
                    P.v(lambda E, kd=kd, cuz=cuz, zb=zb:
                        E.tensor_tensor(shb(zb, 2048 + 256 * kd, 256),
                                        zs_slice(kd, cuz, 256),
                                        shb(zb, 256 * kd, 256), ALU.subtract),
                        reads=(AZB[kd], zn), wnd=(zn,))

            emit_zsp(0)

            for w in range(NW):
                cu = 1 + WIN * w
                kcs = kcs_of(w)

                for nm, hbm, toff, scol, sb_ in (("A1q", c1q_h, 0, S_MISC + 32, "qscol"),
                                                 ("A2q", c2q_h, 256, S_MISC + 33, "qsscol"),
                                                 ("A1k", c1k_h, 512, S_MISC + 34, "kscol"),
                                                 ("A2k", c2k_h, 768, S_MISC + 35, "ksscol")):
                    P.dma(lambda E, hbm=hbm, toff=toff, w=w:
                          E.dma_start(out=sm(S_TAB + toff, 256), in_=hbm[:, WIN * w:WIN * w + 256]),
                          writes=("t" + nm,))
                    P.v(lambda E, toff=toff, scol=scol:
                        E.tensor_single_scalar(sm(S_TAB + toff, 256), sm(S_TAB + toff, 256),
                                               sm(scol, 1), ALU.mult),
                        reads=("t" + nm, sb_), writes=("t" + nm,))


                zb = O_WO + 2048 * (w % 2)
                zn = f"zsp{w % 2}"
                for which, wbuf, wname in (("q", O_WQ, "wq"), ("k", O_WK, "wk")):
                    scol0 = 0 if which == "q" else 256

                    def emit_e8(i, scol0=scol0):
                        sqa = sm((S_T1, S_T2)[i % 2], 256)
                        P.t(lambda E, i=i, sqa=sqa, scol0=scol0:
                            E.matmul(pb[2][0:8, scol0:scol0 + 256], E8[:, 8 * i:8 * i + 8], sqa,
                                     start=(i == 0), stop=(i == 3)),
                            reads=(("sqA", "sqB")[i % 2], "E8"), writes=("pb2",) if i == 0 else (),
                            wnd=("pb2",) if i else ())

                    for i in range(4):
                        bank = PBN[i % 2]
                        nmm = 0
                        for kd in range(NDT):
                            for wd, xd in ((0, 0), (0, 2048), (4096, 0)):
                                P.t(lambda E, i=i, kd=kd, wbuf=wbuf, wd=wd, xd=xd, nmm=nmm, zb=zb:
                                    E.matmul(pb[i % 2][:, 0:256],
                                             shb(wbuf, wd + kd * 512 + 128 * i, 128),
                                             shb(zb, xd + kd * 256, 256),
                                             start=(nmm == 0), stop=(nmm == 23)),
                                    reads=(wname, zn), writes=(bank,) if nmm == 0 else (),
                                    wnd=(bank,) if nmm else ())
                                nmm += 1
                        dst = sh(O_QS + 256 * i, 256) if which == "q" else sh(O_KR + (w % 2) * 256 + i * 512, 256)
                        dstb = "qsb" if which == "q" else f"kr{w % 2}"
                        P.s(lambda E, dst=dst, i=i: E.copy(dst, pb[i % 2][:, 0:256]),
                            reads=(bank,), writes=(dstb,) if i == 0 else (), wnd=(dstb,) if i else ())
                        sqa = sm((S_T1, S_T2)[i % 2], 256)
                        P.g(lambda E, sqa=sqa, dst=dst: E.tensor_tensor(sqa, dst, dst, ALU.mult),
                            reads=(dstb,), writes=(("sqA", "sqB")[i % 2],))
                        if i >= 1:
                            emit_e8(i - 1)
                    emit_e8(3)
                    rsa = sm(S_STA + scol0, 256, 0, 8)
                    P.s(lambda E, rsa=rsa, scol0=scol0:
                        E.activation(rsa, pb[2][0:8, scol0:scol0 + 256], AF.Sqrt),
                        reads=("pb2",), writes=("stA",))
                    P.v(lambda E, rsa=rsa: E.tensor_scalar_max(rsa, rsa, 1e-12),
                        reads=(), writes=("stA",))
                    P.v(lambda E, rsa=rsa: E.reciprocal(rsa, rsa), reads=(), writes=("stA",))

                def emit_rot(which, i):
                    scol0 = 0 if which == "q" else 256
                    rbk = 6 if which == "q" else 7
                    rsa = sm(S_STA + scol0, 256, 0, 8)
                    t_a1 = S_TAB + (0 if which == "q" else 512)
                    t_a2 = S_TAB + (256 if which == "q" else 768)
                    a1b = "tA1q" if which == "q" else "tA1k"
                    a2b = "tA2q" if which == "q" else "tA2k"
                    P.t(lambda E, i=i, rsa=rsa, rbk=rbk:
                        E.matmul(pb[rbk][:, 0:256], R8[0:8, 128 * i:128 * (i + 1)], rsa,
                                 start=True, stop=True),
                        reads=("stA", "R8"), writes=(PBN[rbk],))
                    raww = (O_QS + 256 * i) if which == "q" else (O_KR + (w % 2) * 256 + i * 512)
                    dst = sh(raww, 256)
                    dstb = "qsb" if which == "q" else f"kr{w % 2}"
                    w0 = sm(S_STC + scol0, 256)
                    w0b = "stCq" if which == "q" else "stCk"
                    P.v(lambda E, dst=dst, w0=w0, rbk=rbk: E.tensor_tensor(w0, dst, pb[rbk][:, 0:256], ALU.mult),
                        reads=(dstb, PBN[rbk]), writes=(w0b,))
                    t1 = sm(S_T1, 256)
                    t2 = sm(S_T2, 256)
                    P.v(lambda E, t1=t1, w0=w0: E.stream_shuffle(t1, w0, SHUF),
                        reads=(w0b,), writes=("sqA",))
                    P.v(lambda E, t1=t1, t2=t2, t_a2=t_a2:
                        E.tensor_tensor(t2, t1, sm(t_a2, 256), ALU.mult),
                        reads=("sqA", a2b), writes=("sqB",))
                    P.v(lambda E, w0=w0, t_a1=t_a1:
                        E.tensor_tensor(w0, w0, sm(t_a1, 256), ALU.mult),
                        reads=(w0b, a1b), writes=(w0b,))
                    P.v(lambda E, w0=w0, t2=t2: E.tensor_tensor(w0, w0, t2, ALU.add),
                        reads=("sqB", w0b), writes=(w0b,))
                    P.s(lambda E, raww=raww, w0=w0: E.copy(shb(raww, 0, 256), w0),
                        reads=(w0b,), writes=(dstb,))
                    P.v(lambda E, raww=raww, w0=w0:
                        E.tensor_tensor(shb(raww + 128, 0, 256), w0,
                                        shb(raww, 0, 256), ALU.subtract),
                        reads=(w0b, dstb), wnd=(dstb,))

                nkc = len(kcs)
                kc0 = kcs[0]

                def emit_v_mms(tc, kds, first, last):
                    bank = PBN[tc % 2]
                    for kd in kds:
                        for wd, xd in ((0, 0), (4096, 0), (0, 2048)):
                            P.t(lambda E, tc=tc, kd=kd, wd=wd, xd=xd, zb=zb:
                                E.matmul(pb[tc % 2][:, 0:512],
                                         shb(zb, xd + kd * 256 + 128 * tc, 128),
                                         shb(O_WV, wd + kd * 512, 512),
                                         start=(kd == 0 and wd == 0 and xd == 0),
                                         stop=(kd == NDT - 1 and xd == 2048)),
                                reads=("wv", zn),
                                writes=(bank,) if (first and kd == kds[0] and wd == 0 and xd == 0) else (),
                                wnd=(bank,) if not (first and kd == kds[0] and wd == 0 and xd == 0) else ())

                def emit_v_evac(tc):
                    bank = PBN[tc % 2]
                    vslot = (w % 2) * 2 + tc
                    dstv = SH[:, O_VR + 520 * vslot:O_VR + 520 * vslot + 520]\
                        .rearrange("p (h c) -> p h c", h=8)[:, :, 0:64]
                    P.s(lambda E, dstv=dstv, tc=tc:
                        E.copy(dstv, pb[tc % 2][:, 0:512].rearrange("p (h c) -> p h c", h=8)),
                        reads=(bank,), writes=(f"vr{vslot}",))

                def emit_qk_slot(par, i):
                    sbase = O_SS
                    ssn = "ss"
                    r0 = 64 * par
                    tp = (64 * par, 0)
                    for kc in kcs:
                        b_ = (w - 1 + kc // 2) % 2
                        kbase = O_KR + b_ * 256 + i * 512
                        bank = PBN[4 + kc // 2]
                        for pi2, (wd, xd) in enumerate(((0, 0), (0, 1), (1, 0))):
                            P.t(lambda E, i=i, kc=kc, r0=r0, tp=tp, kbase=kbase,
                                wd=wd, xd=xd, pi2=pi2:
                                E.matmul(pb[4 + kc // 2][:, 256 * (kc % 2):256 * (kc % 2) + 256],
                                         shb(kbase + 128 * wd, 128 * (kc % 2), 128, r0, r0 + 64),
                                         shb(O_QS + 256 * i + 128 * xd, 0, 256, r0, r0 + 64),
                                         start=(pi2 == 0), stop=(pi2 == 2), tile_position=tp),
                                reads=(f"kr{b_}", "qsb"), writes=(bank,) if pi2 == 0 else (),
                                wnd=(bank,) if pi2 else ())
                    for half in ((0, 1) if w else (1,)):
                        bank = PBN[4 + half]
                        first = (i == 0 and half == (0 if w else 1))
                        P.s(lambda E, i=i, half=half, sbase=sbase:
                            E.copy(sh(sbase + 1024 * i + 512 * half, 512),
                                   pb[4 + half][:, 0:512]),
                            reads=(bank,),
                            writes=(ssn,) if first else (),
                            wnd=(ssn,))
                    hoff = sbase + 1024 * i + (0 if w else 512)
                    ap_h = SH[:, hoff:hoff + 256 * nkc].rearrange("p (k q) -> p k q", k=nkc)
                    P.g(lambda E, ap_h=ap_h, nkc=nkc, kc0=kc0:
                        E.affine_select(ap_h, ap_h, [[-128, nkc], [1, 256]],
                                        ALU.is_ge, regs["neg"], base=256 - 128 * kc0,
                                        channel_multiplier=-1),
                        reads=(ssn,), writes=(ssn,))
                    P.g(lambda E, ap_h=ap_h, nkc=nkc, kc0=kc0:
                        E.affine_select(ap_h, ap_h, [[128, nkc], [-1, 256]],
                                        ALU.is_ge, regs["neg"], base=128 * kc0,
                                        channel_multiplier=1),
                        reads=(ssn,), writes=(ssn,))
                    P.s(lambda E, ap_h=ap_h: E.activation(ap_h, ap_h, AF.Exp,
                                                          bias=negc, scale=8.0),
                        reads=(ssn, "negc"), writes=(ssn,))

                def emit_sums_pv(par):
                    ssn = "ss"
                    stv = SH[:, O_SS:O_SS + 4096].rearrange("p (h k q) -> p h k q", h=4, k=4)
                    for gi, (h0, bnk) in enumerate(((0, 2), (2, 3))):
                        for ki, kc in enumerate(kcs):
                            P.t(lambda E, h0=h0, kc=kc, bnk=bnk, ki=ki, nkc=nkc, stv=stv:
                                E.matmul(pb[bnk][0:1, 0:512], ONESC[:],
                                         stv[:, h0:h0 + 2, kc, :],
                                         start=(ki == 0), stop=(ki == nkc - 1)),
                                reads=(ssn, "ONESC"), writes=(PBN[bnk],) if ki == 0 else (),
                                wnd=(PBN[bnk],) if ki else ())
                    recA = sm(S_STB, 512, 0, 1)
                    recB = sm(S_STC, 512, 0, 1)
                    P.v(lambda E: E.reciprocal(recA, pb[2][0:1, 0:512]), reads=("pb2",), writes=("stB",))
                    P.v(lambda E: E.reciprocal(recB, pb[3][0:1, 0:512]), reads=("pb3",), writes=("stC",))
                    for pi in range(2):
                        os_slot = pi if par == 0 else 2 + pi
                        rec = recA if pi == 0 else recB
                        rb = "stB" if pi == 0 else "stC"
                        P.t(lambda E, rec=rec: E.matmul(pb[4][:, 0:256], INDLO,
                                                        rec[:, 0:256], start=True, stop=False),
                            reads=(rb, "IND"), writes=("pb4",))
                        P.t(lambda E, rec=rec: E.matmul(pb[4][:, 0:256], INDHI,
                                                        rec[:, 256:512], start=False, stop=True),
                            reads=(rb, "IND"), wnd=("pb4",))
                        repsb = sm(S_T1, 256)
                        P.s(lambda E, repsb=repsb: E.copy(repsb, pb[4][:, 0:256]),
                            reads=("pb4",), writes=("sqA",))
                        for sub in range(2):
                            slot = 2 * pi + sub
                            hl = par + 2 * slot
                            tp = (0, 64 * sub)
                            prange = (64 * sub, 64 * sub + 64)
                            for ki, kc in enumerate(kcs):
                                b_ = (w - 1 + kc // 2) % 2
                                vslot = b_ * 2 + (kc % 2)
                                P.t(lambda E, hl=hl, kc=kc, vslot=vslot, slot=slot, tp=tp,
                                    prange=prange, ki=ki, nkc=nkc, stv=stv:
                                    E.matmul(pb[5][prange[0]:prange[1], 0:256],
                                             sh(O_VR + 520 * vslot + 65 * hl, 64),
                                             stv[:, slot, kc, :],
                                             start=(ki == 0), stop=(ki == nkc - 1),
                                             tile_position=tp),
                                    reads=(f"vr{vslot}", ssn), writes=("pb5",) if ki == 0 else (),
                                    wnd=("pb5",) if ki else ())
                        ot = sm(S_T2, 256)
                        P.v(lambda E, ot=ot, repsb=repsb:
                            E.tensor_tensor(ot, pb[5][:, 0:256], repsb, ALU.mult),
                            reads=("pb5", "sqA"), writes=("sqB",))
                        P.s(lambda E, os_slot=os_slot, ot=ot:
                            E.copy(shb(O_OS, 256 * os_slot, 256), ot),
                            reads=("sqB",), writes=(f"os{os_slot}",))
                        P.v(lambda E, os_slot=os_slot, ot=ot:
                            E.tensor_tensor(shb(O_OS + 512, 256 * os_slot, 256), ot,
                                            shb(O_OS, 256 * os_slot, 256), ALU.subtract),
                            reads=("sqB", f"os{os_slot}"), wnd=(f"os{os_slot}",))

                emit_rot("q", 0)
                emit_rot("k", 0)
                for i in range(4):
                    if i + 1 < 4:
                        emit_rot("q", i + 1)
                        emit_rot("k", i + 1)
                    emit_qk_slot(0, i)
                    emit_v_mms(0, [2 * i, 2 * i + 1], first=(i == 0), last=(i == 3))
                emit_v_evac(0)
                emit_v_mms(1, list(range(NDT)), first=True, last=True)
                emit_v_evac(1)
                emit_sums_pv(0)
                P.pe_barrier()
                for i in range(4):
                    emit_qk_slot(1, i)
                P.pe_barrier()
                emit_sums_pv(1)
                if w + 1 < NW:
                    emit_zsp(w + 1)

                if hg == 1:
                    P.dma(lambda E, w=w: E.dma_start(out=sh(O_SS, 2048), in_=wosp[:, 2048 * w:2048 * (w + 1)]),
                          reads=("wosp",), writes=("ss",))
                for dc in range(NDT):
                    bank = PBN[dc % 2]
                    nmm = 0
                    for kt in range(4):
                        for wd, xd in ((0, 0), (0, 1024), (4096, 0)):
                            P.t(lambda E, dc=dc, kt=kt, wd=wd, xd=xd, nmm=nmm:
                                E.matmul(pb[dc % 2][:, 0:256],
                                         shb(O_WOH, wd + kt * 1024 + 128 * dc, 128),
                                         shb(O_OS, xd + 256 * kt, 256),
                                         start=(nmm == 0), stop=(nmm == 11)),
                                reads=("woh", "wol", f"os{kt}"), writes=(bank,) if nmm == 0 else (),
                                wnd=(bank,) if nmm else ())
                            nmm += 1
                    stage = sh(O_SS + 256 * dc, 256)
                    if hg == 0:
                        P.s(lambda E, stage=stage, dc=dc: E.copy(stage, pb[dc % 2][:, 0:256]),
                            reads=(bank,), writes=("ss",) if dc == 0 else (), wnd=("ss",))
                    else:
                        P.v(lambda E, dc=dc, cu=cu:
                            E.scalar_tensor_tensor(az(dc, cu, 256), pb[dc % 2][:, 0:256], 1.0,
                                                   az(dc, cu, 256), ALU.mult, ALU.add),
                            reads=(bank, AZB[dc]), writes=(AZB[dc],))
                        P.v(lambda E, dc=dc, cu=cu, stage=stage:
                            E.tensor_tensor(az(dc, cu, 256), az(dc, cu, 256), stage, ALU.add),
                            reads=("ss", AZB[dc]), writes=(AZB[dc],))
                if hg == 0:
                    P.dma(lambda E, w=w: E.dma_start(out=wosp[:, 2048 * w:2048 * (w + 1)], in_=sh(O_SS, 2048)),
                          reads=("ss",), writes=("wosp",))

        if f"l{l}a" in dbg_out:
            P.dma(lambda E, l=l: E.dma_start(out=dbg_out[f"l{l}a"], in_=AZ[:]), reads=AZB,
                  writes=(f"dbgl{l}a",))

        P.fence(ATTN_BUFS, FFN_BUFS, sm(S_MISC + 169, 1))

        def emit_ln(tb):
            cu = 1 + 512 * tb
            nslot = tb % 2
            ntb = F_NT + 4096 * nslot
            ntn = f"ntq{nslot}"

            def zs2(dt_, cu=cu):
                off = 0 if dt_ < 4 else -1
                return az(dt_, cu + off, 512)

            def wrn(dt_, src, ntb=ntb, ntn=ntn):
                ap_, b_ = src
                t1 = sm(S_T1, 512)
                t2 = sm(S_T2, 512)
                gc = sm(S_BIAS + 68 + dt_, 1)
                bc = sm(S_BIAS + 76 + dt_, 1)
                P.v(lambda E, ap_=ap_, t1=t1: E.tensor_tensor(t1, ap_, pb[6][:, 0:512], ALU.subtract),
                    reads=(b_, "pb6"), writes=("sqA",))
                P.v(lambda E, t1=t1, t2=t2: E.tensor_tensor(t2, t1, pb[7][:, 0:512], ALU.mult),
                    reads=("sqA", "pb7"), writes=("sqB",))
                P.v(lambda E, t1=t1, t2=t2, gc=gc, bc=bc:
                    E.tensor_scalar(t1, t2, gc, bc, ALU.mult, ALU.add),
                    reads=("sqB", "lngc", "lnbfc"), writes=("sqA",))
                P.s(lambda E, dt_=dt_, t1=t1, ntb=ntb: E.copy(shb(ntb, 512 * dt_, 512), t1),
                    reads=("sqA",), writes=(ntn,) if dt_ == 0 else (), wnd=(ntn,))
                P.v(lambda E, dt_=dt_, t1=t1, ntb=ntb:
                    E.tensor_tensor(shb(ntb + 2048, 512 * dt_, 512), t1,
                                    shb(ntb, 512 * dt_, 512), ALU.subtract),
                    reads=("sqA", ntn), wnd=(ntn,))
            ln_block(lambda dt_: (zs2(dt_), AZB[dt_]), 512, wrn)

        def w1_load(jcp):
            slot = jcp % 2
            P.dma(lambda E, jcp=jcp, slot=slot, l=l:
                  E.dma_start(out=sh(F_W1S + 2048 * slot, 2048).rearrange("p (k m) -> p k m", k=8),
                              in_=w1_h[l, :, 256 * jcp:256 * jcp + 256].rearrange("(k p) m -> p k m", p=128)),
                  writes=(f"w1b{slot}",))
            P.s(lambda E, slot=slot: E.copy(shb(F_W1H + 2048 * slot, 0, 2048),
                                            sh(F_W1S + 2048 * slot, 2048)),
                reads=(f"w1b{slot}",), writes=(f"w1h{slot}",))
            P.v(lambda E, slot=slot:
                E.tensor_tensor(shb(F_W1H + 2048 * slot + 1024, 0, 2048),
                                sh(F_W1S + 2048 * slot, 2048),
                                shb(F_W1H + 2048 * slot, 0, 2048), ALU.subtract),
                reads=(f"w1b{slot}", f"w1h{slot}"), wnd=(f"w1h{slot}",))

        def w2_load(jc):
            slot = jc % 2
            stg = sh(F_W2S, 1024) if slot == 0 else sm(S_TAB, 1024)
            stn = "w2b0" if slot == 0 else "w2c1"
            P.dma(lambda E, jc=jc, l=l, stg=stg:
                  E.dma_start(out=stg,
                              in_=w2_h[l, 128 * jc:128 * jc + 128, :]),
                  writes=(stn,))
            P.s(lambda E, slot=slot, stg=stg: E.copy(shb(F_W2H + 1024 * slot, 0, 1024),
                                                     stg),
                reads=(stn,), writes=(f"w2h{slot}",))
            P.v(lambda E, slot=slot, stg=stg:
                E.tensor_tensor(shb(F_W2H + 1024 * slot + 512, 0, 1024),
                                stg,
                                shb(F_W2H + 1024 * slot, 0, 1024), ALU.subtract),
                reads=(stn, f"w2h{slot}"), wnd=(f"w2h{slot}",))

        emit_ln(0)
        w1_load(0)
        w1_load(1)
        for tb in range(4):
            cu = 1 + 512 * tb
            nslot = tb % 2
            ntb = F_NT + 4096 * nslot
            ntn = f"ntq{nslot}"

            for jcp in range(NJC):
                slot = jcp % 2
                for half in range(2):
                    bk = 2 * slot + half
                    bank = PBN[bk]
                    nmm = 0
                    for kd in range(NDT):
                        for wd, xd in ((0, 0), (0, 4096), (2048, 0)):
                            P.t(lambda E, slot=slot, bk=bk, kd=kd, wd=wd, xd=xd, nmm=nmm, ntb=ntb:
                                E.matmul(pb[bk][:, 0:512],
                                         shb(F_W1H + 2048 * slot, wd + 256 * kd + 128 * (bk % 2), 128),
                                         shb(ntb, xd + 512 * kd, 512),
                                         start=(nmm == 0), stop=(nmm == 23)),
                                reads=(f"w1h{slot}", ntn), writes=(bank,) if nmm == 0 else (),
                                wnd=(bank,) if nmm else ())
                            nmm += 1
                gg = sm(S_T1, 512)
                mt = sm(S_T2, 512)
                P.s(lambda E, gg=gg, jcp=jcp, slot=slot:
                    E.activation(gg, pb[2 * slot + 1][:, 0:512], AF.Gelu, bias=sm(S_BIAS + 24 + 2 * jcp + 1, 1)),
                    reads=(PBN[2 * slot + 1], "b1c"), writes=("sqA",))
                P.v(lambda E, jcp=jcp, gg=gg, mt=mt, slot=slot:
                    E.scalar_tensor_tensor(mt, pb[2 * slot][:, 0:512], sm(S_BIAS + 24 + 2 * jcp, 1),
                                           gg, ALU.add, ALU.mult),
                    reads=(PBN[2 * slot], "sqA", "b1c"), writes=("sqB",))
                P.s(lambda E, jcp=jcp, mt=mt: E.copy(shb(F_MB, 512 * jcp, 512), mt),
                    reads=("sqB",), writes=("mb",) if jcp == 0 else (), wnd=("mb",))
                P.v(lambda E, jcp=jcp, mt=mt:
                    E.tensor_tensor(shb(F_MB + 5632, 512 * jcp, 512), mt,
                                    shb(F_MB, 512 * jcp, 512), ALU.subtract),
                    reads=("sqB", "mb"), wnd=("mb",))
                if jcp + 2 < NJC:
                    w1_load(jcp + 2)
                elif jcp == NJC - 2:
                    w2_load(0)
                else:
                    w2_load(1)

            if tb < 3:
                emit_ln(tb + 1)
            for jc in range(NJC):
                slot = jc % 2
                for dc in range(NDT):
                    bank = PBN[dc]
                    for pi, (wd, xd) in enumerate(((0, 0), (0, 11264), (1024, 0))):
                        P.t(lambda E, jc=jc, slot=slot, dc=dc, wd=wd, xd=xd, pi=pi:
                            E.matmul(pb[dc][:, 0:512],
                                     shb(F_W2H + 1024 * slot, wd + 128 * dc, 128),
                                     shb(F_MB, xd + 512 * jc, 512),
                                     start=(jc == 0 and pi == 0), stop=(jc == NJC - 1 and pi == 2)),
                            reads=(f"w2h{slot}", "mb"), writes=(bank,) if (jc == 0 and pi == 0) else (),
                            wnd=(bank,) if not (jc == 0 and pi == 0) else ())
                if jc + 2 < NJC:
                    w2_load(jc + 2)
                elif tb < 3 and jc == NJC - 2:
                    w1_load(0)
                elif tb < 3 and jc == NJC - 1:
                    w1_load(1)
            for dc in range(NDT):
                bank = PBN[dc]
                P.v(lambda E, dc=dc, cu=cu:
                    E.scalar_tensor_tensor(az(dc, cu, 512), pb[dc][:, 0:512],
                                           sm(S_BIAS + 84 + dc, 1),
                                           az(dc, cu, 512), ALU.add, ALU.add),
                    reads=(bank, "b2c", AZB[dc]), writes=(AZB[dc],))

        if f"l{l}f" in dbg_out:
            P.dma(lambda E, l=l: E.dma_start(out=dbg_out[f"l{l}f"], in_=AZ[:]), reads=AZB,
                  writes=(f"dbgl{l}f",))

    P.fence(ATTN_BUFS + FFN_BUFS + RWKV_BUFS + ("wq", "wk"), ("x1r", "ob0", "ob1"),
            sm(S_MISC + 170, 1))
    out_toks = []
    if do_final:
        if do_rwkv:
            P.dma(lambda E: E.dma_start(out=sh(0, NDT * COL), in_=x1sp),
                  reads=("x1sp",), writes=("x1r",), extra=P.deps_of(*ATTN_BUFS, *FFN_BUFS))
            P.v(lambda E: E.tensor_tensor(AZ[:], AZ[:], sh(0, NDT * COL), ALU.add),
                reads=("x1r",) + AZB, writes=AZB)
        for j in range(16):
            slot = j % 2
            ob = sh(F_NT + 1024 * slot, 1024)
            obn = f"ob{slot}"
            for g4 in range(2):
                bank = PBN[g4]
                for q in range(4):
                    P.t(lambda E, j=j, g4=g4, q=q:
                        E.transpose(pb[g4][:, 128 * q:128 * (q + 1)],
                                    az(g4 * 4 + q, 1 + 128 * j, 128), ID[:]),
                        reads=(AZB[g4 * 4 + q], "ID"), writes=(bank,) if q == 0 else (),
                        wnd=(bank,) if q else ())
                P.s(lambda E, ob=ob, g4=g4:
                    E.copy(ob[:, 512 * g4:512 * (g4 + 1)], pb[g4][:, 0:512]),
                    reads=(bank,), writes=(obn,) if g4 == 0 else (), wnd=(obn,))
            out_toks.append(
                P.dma(lambda E, j=j, ob=ob: E.dma_start(out=out_h[128 * j:128 * (j + 1), :], in_=ob),
                      reads=(obn,), writes=(f"outd{j}",)))
    else:
        for j in range(16):
            slot = j % 2
            ob = sh(F_NT + 1024 * slot, 1024)
            obn = f"ob{slot}"
            for g4 in range(2):
                bank = PBN[g4]
                for q in range(4):
                    P.t(lambda E, j=j, g4=g4, q=q:
                        E.transpose(pb[g4][:, 128 * q:128 * (q + 1)],
                                    az(g4 * 4 + q, 1 + 128 * j, 128), ID[:]),
                        reads=(AZB[g4 * 4 + q], "ID"), writes=(bank,) if q == 0 else (),
                        wnd=(bank,) if q else ())
                P.s(lambda E, ob=ob, g4=g4:
                    E.copy(ob[:, 512 * g4:512 * (g4 + 1)], pb[g4][:, 0:512]),
                    reads=(bank,), writes=(obn,) if g4 == 0 else (), wnd=(obn,))
            out_toks.append(
                P.dma(lambda E, j=j, ob=ob: E.dma_start(out=out_h[128 * j:128 * (j + 1), :], in_=ob),
                      reads=(obn,), writes=(f"outd{j}",)))

    for name in dbg:
        out_toks.extend(P.deps_of(f"dbg{name}"))
    P.final_wait("sync", out_toks)
    P.emit()
    es.close()
    return nc



def _bf16():
    import ml_dtypes
    return ml_dtypes.bfloat16


def _dh_interleave():
    dh = np.empty(64, np.int64)
    dh[0::2] = np.arange(32)
    dh[1::2] = np.arange(32) + 32
    return dh


def _host_constants():
    dh = _dh_interleave()
    t = np.arange(NT, dtype=np.float64)
    half = np.arange(0, 64, 2, dtype=np.float64)
    inv_freq = 1.0 / 10000.0 ** (half / 64.0)
    fh = t[:, None] * inv_freq[None, :]
    freqs = np.concatenate([fh, fh], -1)
    base_scale = (half + 0.4 * 64) / (1.4 * 64)
    power = (t - NT // 2) / (WIN // 2)
    sc_h = base_scale[None, :] ** power[:, None]
    scale = np.concatenate([sc_h, sc_h], -1)
    cos, sin = np.cos(freqs), np.sin(freqs)
    p = np.arange(128)
    dhp = dh[p % 64]
    sign = np.where((p % 64) % 2 == 0, -1.0, 1.0)
    c1q = (cos[:, dhp] * scale[:, dhp]).T
    c2q = (sign[:, None] * (sin[:, dhp] * scale[:, dhp]).T)
    inv = 1.0 / scale
    c1k = (cos[:, dhp] * inv[:, dhp]).T
    c2k = (sign[:, None] * (sin[:, dhp] * inv[:, dhp]).T)
    e8 = np.zeros((128, 4, 8), np.float32)
    r8 = np.zeros((8, 4, 128), np.float32)
    for i in range(4):
        for pp in range(128):
            c = 2 * i + (1 if pp >= 64 else 0)
            e8[pp, i, c] = 1.0
            r8[c, i, pp] = 1.0
    ind2 = np.zeros((1, 2, 128), np.float32)
    ind2[0, 0, 0:64] = 1.0
    ind2[0, 1, 64:128] = 1.0
    return {
        "c1q": np.ascontiguousarray(c1q, np.float32),
        "c2q": np.ascontiguousarray(c2q, np.float32),
        "c1k": np.ascontiguousarray(c1k, np.float32),
        "c2k": np.ascontiguousarray(c2k, np.float32),
        "e8": e8, "r8": r8, "ind2": ind2,
        "onesr": np.ones((1, 128), np.float32),
        "onesc": np.ones((128, 1), np.float32),
        "ident": np.eye(128, dtype=np.float32),
    }


def _host_weights(ins):
    dh = _dh_interleave()
    qperm = np.empty(D, np.int64)
    for i in range(8):
        for pp in range(128):
            head = 2 * i + (1 if pp >= 64 else 0)
            qperm[i * 128 + pp] = head * 64 + dh[pp % 64]
    wqkv = np.asarray(ins["L0_ffn_Wqkv"])
    wqkvR = np.concatenate([wqkv[:, :, qperm], wqkv[:, :, 1024 + qperm], wqkv[:, :, 2048:]], axis=2)
    woperm = []
    for hg in range(2):
        for (a, b) in ((0, 2), (4, 6), (1, 3), (5, 7)):
            woperm.extend(range((8 * hg + a) * 64, (8 * hg + a) * 64 + 64))
            woperm.extend(range((8 * hg + b) * 64, (8 * hg + b) * 64 + 64))
    woperm = np.array(woperm)
    woR = np.asarray(ins["L0_ffn_Wo"])[:, woperm, :]
    w1 = np.asarray(ins["L0_ffn_W1"])
    b1 = np.asarray(ins["L0_ffn_b1"])
    w2 = np.asarray(ins["L0_ffn_W2"])
    w1p = np.zeros((NL, D, 2 * DI), np.float32)
    b1p = np.zeros((NL, 128, 2 * NJC), np.float32)
    w2p = np.zeros((NL, DI, D), np.float32)
    w2p[:, :2730, :] = w2
    for i in range(NJC):
        a0, a1 = 128 * i, min(128 * (i + 1), 2730)
        n = a1 - a0
        w1p[:, :, 256 * i:256 * i + n] = w1[:, :, a0:a1]
        w1p[:, :, 256 * i + 128:256 * i + 128 + n] = w1[:, :, 2730 + a0:2730 + a1]
        b1p[:, :n, 2 * i] = b1[:, a0:a1]
        b1p[:, :n, 2 * i + 1] = b1[:, 2730 + a0:2730 + a1]
    qsc = np.asarray(ins["L0_ffn_q_scale"])
    ksc = np.asarray(ins["L0_ffn_k_scale"])
    p = np.arange(128)
    dhp = dh[p % 64]
    qs = qsc[:, dhp][:, :, None]
    ks = ksc[:, dhp][:, :, None]
    dhs = dh[(p % 64) ^ 1]
    qss = qsc[:, dhs][:, :, None]
    kss = ksc[:, dhs][:, :, None]
    return {
        "wqkv": np.ascontiguousarray(wqkvR, np.float32),
        "wo": np.ascontiguousarray(woR, np.float32),
        "w1": w1p, "w2": w2p, "b1c": b1p,
        "w1b": w1p.view(_bf16()), "w2b": w2p.view(_bf16()),
        "wob": np.ascontiguousarray(woR, np.float32).view(_bf16()),
        "lng": np.ascontiguousarray(np.asarray(ins["L0_ffn_ln_g"]).reshape(NL, NDT, 128).transpose(0, 2, 1)),
        "lnbf": np.ascontiguousarray(np.asarray(ins["L0_ffn_ln_b"]).reshape(NL, NDT, 128).transpose(0, 2, 1)),
        "b2": np.ascontiguousarray(np.asarray(ins["L0_ffn_b2"]).reshape(NL, NDT, 128).transpose(0, 2, 1)),
        "qs": np.ascontiguousarray(qs, np.float32),
        "qss": np.ascontiguousarray(qss, np.float32),
        "ks": np.ascontiguousarray(ks, np.float32),
        "kss": np.ascontiguousarray(kss, np.float32),
        "qsrow": np.ascontiguousarray(qsc[:, None, :], np.float32),
        "ksrow": np.ascontiguousarray(ksc[:, None, :], np.float32),
        "kvwt": np.ascontiguousarray(
            np.concatenate([np.asarray(ins["L0_attn_key_w"]).T,
                            np.asarray(ins["L0_attn_value_w"]).T], axis=1), np.float32),
        "owt": np.ascontiguousarray(np.asarray(ins["L0_attn_output_w"]).T, np.float32),
        "td": np.ascontiguousarray(np.asarray(ins["L0_attn_time_decay"])[:, None], np.float32),
        "tf": np.ascontiguousarray(np.asarray(ins["L0_attn_time_first"])[:, None], np.float32),
        "gatesf": np.ascontiguousarray(
            np.asarray(ins["L0_attn_time_shift_gates"]).reshape(NDT, 128).T, np.float32),
        "lnw": np.ascontiguousarray(np.asarray(ins["L0_attn_ln_w"]).reshape(NDT, 128).T, np.float32),
        "lnb": np.ascontiguousarray(np.asarray(ins["L0_attn_ln_b"]).reshape(NDT, 128).T, np.float32),
    }


def kernel(**inputs):
    x = np.asarray(inputs["L0_attn_x"])
    B = x.shape[0]
    consts = _host_constants()
    weights = _host_weights(inputs)
    nc = build_program()
    in_maps = []
    for b in range(B):
        m = {"x": np.ascontiguousarray(x[b], np.float32)}
        m.update(consts)
        m.update(weights)
        in_maps.append(m)
    res = run_bass_kernel_spmd(nc, in_maps, core_ids=list(range(B)))
    return np.stack([np.asarray(r["out"]) for r in res.results], axis=0)
